# version 11
# baseline (speedup 1.0000x reference)
import numpy as np
import ml_dtypes
import concourse.bass as bass
import concourse.mybir as mybir
from concourse.alu_op_type import AluOpType
from concourse.bass_utils import run_bass_kernel_spmd

F32 = mybir.dt.float32
BF16 = mybir.dt.bfloat16
I32 = mybir.dt.int32
F16 = mybir.dt.float16
AF = mybir.ActivationFunctionType
OP = AluOpType
AX = mybir.AxisListType

NCORE = 8
T = 8192
TL = T // NCORE
D = 4096
DS = D // NCORE
NKT = DS // 128
KTF = D // 128
NST = 16
RNK = 128
NMIX = RNK + 2 * NST
LC = 1024
NCH = T // LC
NTT = TL // 128
EPS = 1e-5

PHASE_LIMIT = 2
FFN_ON = True
STAGE_ON = True
NCH_LIM = NCH
P0_CUT = 99
DBG = None


class Plan:

    ENGS = ("sync", "vector", "scalar", "tensor", "gpsimd")

    def __init__(self, nc):
        self.nc = nc
        self.ops = {k: [] for k in self.ENGS}
        self.sems = {}
        self.counts = {}

    def sem(self, name):
        if name not in self.sems:
            self.sems[name] = self.nc.semaphore(name).__enter__()
            self.counts[name] = 0
        return self.sems[name]

    def dma(self, eng, out, in_, sem=None, **kw):
        sem_name = sem or {"sync": "DS", "scalar": "DA", "gpsimd": "DG", "vector": "DV"}[eng]
        s = self.sem(sem_name)
        prev = self.counts[sem_name]
        self.counts[sem_name] += 16
        def fn(e, _s=s, _o=out, _i=in_, _kw=kw, _p=prev):
            if _p > 0:
                e.wait_ge(_s, _p)
            e.dma_start(out=_o, in_=_i, **_kw).then_inc(_s, 16)
        self.ops[eng].append(fn)
        return self.counts[sem_name]

    def instr(self, eng, builder):
        sem_name = {"vector": "V", "scalar": "S", "tensor": "TE", "gpsimd": "G"}[eng]
        s = self.sem(sem_name)
        self.counts[sem_name] += 1
        def fn(e, _s=s, _b=builder):
            _b(e).then_inc(_s, 1)
        self.ops[eng].append(fn)
        return self.counts[sem_name]

    def raw(self, eng, builder):
        self.ops[eng].append(lambda e, _b=builder: _b(e))

    def cc(self, builder):
        s = self.sem("CC")
        self.counts["CC"] += 1
        def fn(e, _s=s, _b=builder):
            _b(e).then_inc(_s, 1)
        self.ops["gpsimd"].append(fn)
        return self.counts["CC"]

    def wait(self, eng, sem_name, value):
        s = self.sem(sem_name)
        if value <= 0:
            return
        def fn(e, _s=s, _v=value):
            e.wait_ge(_s, _v)
        self.ops[eng].append(fn)

    def cur(self, sem_name):
        self.sem(sem_name)
        return self.counts[sem_name]


class _PhaseDone(Exception):
    pass


def build_program():
    nc = bass.Bass("TRN2", target_bir_lowering=False)
    nc.detect_race_conditions = False

    x_in = nc.dram_tensor("x_in", [TL, D], F32, kind="ExternalInput").ap()
    alog_in = nc.dram_tensor("alog_in", [DS, NST], F32, kind="ExternalInput").ap()
    dvec_in = nc.dram_tensor("dvec_in", [DS], F32, kind="ExternalInput").ap()
    wxp_in = nc.dram_tensor("wxp_in", [NMIX, D], F32, kind="ExternalInput").ap()
    wdt_in = nc.dram_tensor("wdt_in", [DS, RNK], F32, kind="ExternalInput").ap()
    dtb_in = nc.dram_tensor("dtb_in", [DS], F32, kind="ExternalInput").ap()
    wpk_in = nc.dram_tensor("wpk_in", [DS, D // 2], I32, kind="ExternalInput").ap()
    wsc_in = nc.dram_tensor("wsc_in", [DS, D // 32], F16, kind="ExternalInput").ap()
    out_t = nc.dram_tensor("out_t", [TL, D], F32, kind="ExternalOutput").ap()

    cin_xf = nc.dram_tensor("cin_xf", [NCORE, NTT, NKT * 128, 128], F16, kind="Internal").ap()
    cout_xa = nc.dram_tensor("cout_xa", [NCORE, NCORE, NTT, NKT * 128, 128], F16,
                             kind="Internal", addr_space="Shared").ap()
    xn_loc = nc.dram_tensor("xn_loc", [NCORE, NTT, NKT * 128, 128], F16, kind="Internal").ap()
    cin_xg = nc.dram_tensor("cin_xg", [NTT, 128, KTF * 128], F16, kind="Internal").ap()
    cin_ms = nc.dram_tensor("cin_ms", [NMIX, TL], F16, kind="Internal").ap()
    cout_ms = nc.dram_tensor("cout_ms", [NCORE * NMIX, TL], F16, kind="Internal",
                             addr_space="Shared").ap()
    cin_w = nc.dram_tensor("cin_w", [D, DS], F16, kind="Internal").ap()
    cout_w = nc.dram_tensor("cout_w", [NCORE * D, DS], F16, kind="Internal",
                            addr_space="Shared").ap()
    cin_y = nc.dram_tensor("cin_y", [NCH, NTT, NKT, 128, 128], F16, kind="Internal").ap()
    cout_ya = nc.dram_tensor("cout_ya", [NCH, NCORE, NTT, NKT, 128, 128], F16, kind="Internal",
                             addr_space="Shared").ap()
    s_loc = nc.dram_tensor("s_loc", [NCORE, NTT, NKT, 128, 128], F16, kind="Internal").ap()
    g_buf = nc.dram_tensor("g_buf", [TL, D], F16, kind="Internal").ap()

    p = Plan(nc)
    rg = [list(range(NCORE))]
    pid_cache = {}
    def grab_pid_sync(e):
        pid_cache["sy"] = nc.partition_id([mybir.EngineType.SP])
    def grab_pid_sc(e):
        pid_cache["sc"] = nc.partition_id([mybir.EngineType.Activation])
    p.ops["sync"].append(grab_pid_sync)
    p.ops["scalar"].append(grab_pid_sc)

    import contextlib
    es = contextlib.ExitStack()
    es0 = contextlib.ExitStack()
    es1 = contextlib.ExitStack()
    es2 = contextlib.ExitStack()
    def S(st, *a):
        return st.enter_context(nc.sbuf_tensor(*a))
    def PS(*a):
        return es.enter_context(nc.psum_tensor(*a))
    id16 = S(es, "id16", [128, 128], F16)
    id32 = S(es, "id32", [128, 128], F32)
    a_par = S(es, "a_par", [128, NKT * NST], F32)
    d_par = S(es, "d_par", [128, NKT], F32)
    dtb_par = S(es, "dtb_par", [128, NKT], F32)
    wdtT = S(es, "wdtT", [128, NKT * 128], F16)
    dtinT = S(es, "dtinT", [128, T], F16)
    carry = S(es, "carry", [128, NKT * NST], F32)
    st_a = S(es, "st_a", [128, 16], F32)
    eps_t = S(es, "eps_t", [128, 1], F32)
    xp_t = S(es, "xp_t", [128, NMIX], F32)
    wxpT = S(es0, "wxpT", [128, KTF * NMIX], F16)
    bT_loc = S(es0, "bT_loc", [NST, TL], F16)
    cT_loc = S(es0, "cT_loc", [NST, TL], F16)
    dtinT_loc = S(es0, "dtinT_loc", [128, TL], F16)
    iota_i = S(es0, "iota_i", [128, 128], I32)
    xt = S(es0, "xt", [128, D], F32)
    xnt = S(es0, "xnt", [128, D], F16)
    sq_scr = S(es0, "sq_scr", [128, D], F32)
    xnT_tt = S(es0, "xnT_tt", [128, KTF * 128], F16)
    pk_t = S(es0, "pk_t", [128, D // 2], I32)
    hi_i = S(es0, "hi_i", [128, D // 2], I32)
    lo_f = S(es0, "lo_f", [128, D // 2], F32)
    hi_f = S(es0, "hi_f", [128, D // 2], F32)
    sc_f = S(es0, "sc_f", [128, D // 32], F32)
    sc_exp = S(es0, "sc_exp", [128, D // 2], F32)
    w_bf = S(es0, "w_bf", [128, D], F16)
    wT_pc = S(es0, "wT_pc", [128, KTF * 128], F16)
    ps_ffn = PS("ps_ffn", [128, 1024], F32)
    ps_y = PS("ps_y", [128, LC], F32)
    ps_dt = PS("ps_dt", [128, 512], F32)
    ps_trb = PS("ps_trb", [128, 2048], F16)
    ps_trf = PS("ps_trf", [128, 512], F32)
    block = es.enter_context(nc.Block())
    with es:
     try:

        p.instr("gpsimd", lambda g: g.iota(iota_i[:], pattern=[[1, 128]],
                                           base=0, channel_multiplier=-1))
        p.wait("vector", "G", p.cur("G"))
        vq = p.instr("vector", lambda v: v.tensor_scalar(
            out=id32[:], in0=iota_i[:], scalar1=0, scalar2=None, op0=OP.is_equal))
        p.wait("vector", "V", vq)
        p.instr("vector", lambda v: v.tensor_copy(id16[:], id32[:]))
        p.wait("vector", "V", p.cur("V"))
        p.instr("vector", lambda v: v.memset(eps_t[:], EPS))
        V_ID = p.cur("V")

        p.dma("sync", a_par[:].rearrange("p (kt n) -> p kt n", n=NST), alog_in.rearrange("(kt p) n -> p kt n", p=128))
        p.dma("sync", d_par[:], dvec_in.rearrange("(kt p) -> p kt", p=128), allow_slow_non_contiguous=True)
        p.dma("sync", dtb_par[:], dtb_in.rearrange("(kt p) -> p kt", p=128), allow_slow_non_contiguous=True)
        p.wait("scalar", "DS", p.cur("DS"))
        p.instr("scalar", lambda s: s.activation(out=a_par[:], in_=a_par[:], func=AF.Exp))
        p.wait("vector", "S", p.cur("S"))
        p.instr("vector", lambda v: v.tensor_scalar(
            out=a_par[:], in0=a_par[:], scalar1=-1.0, scalar2=None, op0=OP.mult))
        V_APAR = p.cur("V")

        with nc.sbuf_tensor("wxp_nat", [128, D], F32) as wxp_nat, \
             nc.sbuf_tensor("wxp_nb", [128, D], F16) as wxp_nb, \
             nc.sbuf_tensor("wxp_nb2", [32, D], F16) as wxp_nb2, \
             nc.sbuf_tensor("wdt_nat", [128, NKT * RNK], F32) as wdt_nat, \
             nc.sbuf_tensor("wdt_nb", [128, NKT * RNK], F16) as wdt_nb:
            v1 = p.dma("sync", wxp_nat[:], wxp_in[0:128, :])
            p.wait("vector", "DS", v1)
            p.instr("vector", lambda v: v.tensor_copy(wxp_nb[:], wxp_nat[:]))
            p.wait("sync", "V", p.cur("V"))
            v2 = p.dma("sync", wxp_nat[0:32, :], wxp_in[128:160, :])
            p.wait("vector", "DS", v2)
            p.instr("vector", lambda v: v.tensor_copy(wxp_nb2[:], wxp_nat[0:32, :]))
            p.wait("sync", "V", p.cur("V"))
            v3 = p.dma("sync", wdt_nat[:].rearrange("p (kt r) -> p kt r", kt=NKT), wdt_in.rearrange("(kt p) r -> p kt r", p=128))
            p.wait("vector", "DS", v3)
            p.instr("vector", lambda v: v.tensor_copy(wdt_nb[:], wdt_nat[:]))
            V_WCAST = p.cur("V")
            p.wait("tensor", "V", V_WCAST)
            for kt in range(KTF):
                p.raw("tensor", lambda t, kt=kt: t.transpose(
                    ps_trb[:, 0:128], wxp_nb[:, kt * 128:(kt + 1) * 128], id16[:]))
                te = p.instr("tensor", lambda t, kt=kt: t.transpose(
                    ps_trb[:, 128:160], wxp_nb2[:, kt * 128:(kt + 1) * 128], id16[0:32, 0:32]))
                p.wait("scalar", "TE", te)
                p.instr("scalar", lambda s, kt=kt: s.activation(
                    out=wxpT[:, kt * NMIX:kt * NMIX + 128], in_=ps_trb[:, 0:128], func=AF.Copy))
                se = p.instr("scalar", lambda s, kt=kt: s.activation(
                    out=wxpT[:, kt * NMIX + 128:(kt + 1) * NMIX], in_=ps_trb[:, 128:160], func=AF.Copy))
                p.wait("tensor", "S", se)
            for kt in range(NKT):
                te = p.instr("tensor", lambda t, kt=kt: t.transpose(
                    ps_trb[:, 0:128], wdt_nb[:, kt * RNK:(kt + 1) * RNK], id16[:]))
                p.wait("scalar", "TE", te)
                se = p.instr("scalar", lambda s, kt=kt: s.activation(
                    out=wdtT[:, kt * 128:(kt + 1) * 128], in_=ps_trb[:, 0:128], func=AF.Copy))
                p.wait("tensor", "S", se)

        if P0_CUT == 1:
            for _tt in range(NTT):
                p.dma("gpsimd", out_t[_tt * 128:(_tt + 1) * 128, :],
                      x_in[_tt * 128:(_tt + 1) * 128, :])
            p.wait("gpsimd", "DG", p.cur("DG"))
            p.wait("sync", "DG", p.cur("DG"))
            raise _PhaseDone()

        if DBG == "wdtT":
            p.wait("gpsimd", "S", p.cur("S"))
            p.dma("gpsimd", out_t.flatten()[0:128 * NKT * 128].rearrange("(p n) -> p n", p=128),
                  wdtT[:])
            p.dma("gpsimd", out_t.flatten()[128 * NKT * 128:2 * 128 * NKT * 128]
                  .rearrange("(p n) -> p n", p=128), wxpT[:, 0:NKT * 128])
            p.wait("gpsimd", "DG", p.cur("DG"))
            p.wait("sync", "DG", p.cur("DG"))
            raise _PhaseDone()

        inv_d = 1.0 / D
        ph0 = {"dg_xnT": 0, "te_xp": 0, "vv_xp": 0}
        for tt in range(NTT):
            vx = p.dma("sync", xt[:], x_in[tt * 128:(tt + 1) * 128, :])
            p.wait("vector", "DS", vx)
            p.instr("vector", lambda v: v.reduce_sum(st_a[:, 0:1], xt[:], axis=AX.X))
            p.wait("scalar", "DS", vx)
            p.instr("scalar", lambda s: s.activation(
                out=sq_scr[:], in_=xt[:], func=AF.Square, accum_out=st_a[:, 1:2]))
            p.wait("vector", "S", p.cur("S"))
            p.wait("vector", "V", p.cur("V"))
            p.instr("vector", lambda v: v.tensor_scalar(
                out=st_a[:, 2:3], in0=st_a[:, 0:1], scalar1=-inv_d, scalar2=None, op0=OP.mult))
            p.instr("vector", lambda v: v.tensor_scalar(
                out=st_a[:, 3:4], in0=st_a[:, 1:2], scalar1=inv_d, scalar2=None, op0=OP.mult))
            p.wait("vector", "V", p.cur("V"))
            p.instr("vector", lambda v: v.tensor_tensor(
                out=st_a[:, 4:5], in0=st_a[:, 2:3], in1=st_a[:, 2:3], op=OP.mult))
            p.wait("vector", "V", p.cur("V"))
            vv = p.instr("vector", lambda v: v.tensor_tensor(
                out=st_a[:, 5:6], in0=st_a[:, 3:4], in1=st_a[:, 4:5], op=OP.subtract))
            p.wait("scalar", "V", vv)
            p.instr("scalar", lambda s: s.activation(
                out=st_a[:, 6:7], in_=st_a[:, 5:6], func=AF.Ln, bias=eps_t[:, 0:1]))
            p.wait("scalar", "S", p.cur("S"))
            sv = p.instr("scalar", lambda s: s.activation(
                out=st_a[:, 7:8], in_=st_a[:, 6:7], func=AF.Exp, scale=-0.5))
            if DBG == "st0" and tt == 0:
                p.wait("gpsimd", "S", p.cur("S"))
                p.wait("gpsimd", "V", p.cur("V"))
                p.dma("gpsimd", out_t[0:128, 0:16], st_a[:])
                p.wait("gpsimd", "DG", p.cur("DG"))
                p.wait("sync", "DG", p.cur("DG"))
                raise _PhaseDone()
            p.wait("vector", "S", sv)
            p.wait("vector", "TE", ph0.get("te_xn", 0))
            vxn = p.instr("vector", lambda v: v.tensor_scalar(
                out=xnt[:], in0=xt[:], scalar1=st_a[:, 2:3], scalar2=st_a[:, 7:8],
                op0=OP.add, op1=OP.mult))
            if DBG == "xn":
                p.wait("gpsimd", "V", vxn)
                p.dma("gpsimd", out_t[tt * 128:(tt + 1) * 128, :], xnt[:])
                p.wait("vector", "DG", p.cur("DG"))
            p.wait("tensor", "V", vxn)
            p.wait("scalar", "DG", ph0["dg_xnT"])
            p.wait("scalar", "TE", ph0["te_xp"])
            for kb in range(KTF // 4):
                hb = (kb % 2) * 1024
                for j in range(4):
                    kt = kb * 4 + j
                    fn = (lambda t, kt=kt, j=j, hb=hb:
                          t.transpose(ps_trb[:, hb + j * 128:hb + (j + 1) * 128],
                                      xnt[:, kt * 128:(kt + 1) * 128], id16[:]))
                    if j == 3:
                        te = p.instr("tensor", fn)
                    else:
                        p.raw("tensor", fn)
                p.wait("scalar", "TE", te)
                se = p.instr("scalar", lambda s, kb=kb, hb=hb: s.activation(
                    out=xnT_tt[:, kb * 512:(kb + 1) * 512], in_=ps_trb[:, hb:hb + 512], func=AF.Copy))
                if kb >= 1:
                    p.wait("tensor", "S", p.cur("S") - 1)
            ph0["te_xn"] = te
            SE_XN = p.cur("S")
            p.wait("gpsimd", "S", SE_XN)
            for dsl in range(NCORE):
                p.dma("gpsimd", cin_xf[dsl, tt].rearrange("(ktl p) t -> p ktl t", p=128),
                      xnT_tt[:, dsl * 512:(dsl + 1) * 512].rearrange(
                          "p (ktl t) -> p ktl t", ktl=NKT))
            p.dma("gpsimd", cin_xg[tt], xnT_tt[:])
            ph0["dg_xnT"] = p.cur("DG")
            p.wait("tensor", "S", SE_XN)
            p.wait("tensor", "V", ph0["vv_xp"])
            for kt in range(KTF):
                fn = (lambda t, kt=kt: t.matmul(
                    ps_dt[:, 0:NMIX], xnT_tt[:, kt * 128:(kt + 1) * 128],
                    wxpT[:, kt * NMIX:(kt + 1) * NMIX],
                    start=(kt == 0), stop=(kt == KTF - 1)))
                if kt == KTF - 1:
                    te = p.instr("tensor", fn)
                else:
                    p.raw("tensor", fn)
            ph0["te_xp"] = te
            p.wait("vector", "TE", te)
            vv = p.instr("vector", lambda v: v.tensor_copy(xp_t[:], ps_dt[:, 0:NMIX]))
            ph0["vv_xp"] = vv
            p.wait("tensor", "V", vv)
            p.raw("tensor", lambda t: t.transpose(ps_trf[:, 0:128], xp_t[:, 0:RNK], id32[:]))
            p.raw("tensor", lambda t: t.transpose(ps_trf[0:NST, 128:256], xp_t[:, RNK:RNK + NST], id32[:]))
            te = p.instr("tensor", lambda t: t.transpose(ps_trf[0:NST, 256:384], xp_t[:, RNK + NST:NMIX], id32[:]))
            p.wait("scalar", "TE", te)
            p.instr("scalar", lambda s, tt=tt: s.activation(
                out=dtinT_loc[:, tt * 128:(tt + 1) * 128], in_=ps_trf[:, 0:128], func=AF.Copy))
            p.instr("scalar", lambda s, tt=tt: s.activation(
                out=bT_loc[:, tt * 128:(tt + 1) * 128], in_=ps_trf[0:NST, 128:256], func=AF.Copy))
            se = p.instr("scalar", lambda s, tt=tt: s.activation(
                out=cT_loc[:, tt * 128:(tt + 1) * 128], in_=ps_trf[0:NST, 256:384], func=AF.Copy))
            if DBG == "xp":
                p.wait("gpsimd", "V", p.cur("V"))
                p.dma("gpsimd", out_t[tt * 128:(tt + 1) * 128, 0:NMIX], xp_t[:])
                p.wait("vector", "DG", p.cur("DG"))
            p.wait("tensor", "S", se)
            p.wait("sync", "V", vxn)

        if P0_CUT == 2:
            for _tt in range(NTT):
                p.dma("gpsimd", out_t[_tt * 128:(_tt + 1) * 128, :],
                      x_in[_tt * 128:(_tt + 1) * 128, :])
            p.wait("gpsimd", "DG", p.cur("DG"))
            p.wait("sync", "DG", p.cur("DG"))
            raise _PhaseDone()

        p.wait("gpsimd", "S", p.cur("S"))
        p.dma("gpsimd", cin_ms[0:RNK, :], dtinT_loc[:])
        p.dma("gpsimd", cin_ms[RNK:RNK + NST, :], bT_loc[:])
        p.dma("gpsimd", cin_ms[RNK + NST:NMIX, :], cT_loc[:])
        p.wait("gpsimd", "DG", p.cur("DG"))
        CC_XN = p.cc(lambda g: g.collective_compute(
            "AllGather", OP.bypass, ins=[cin_xf], outs=[cout_xa], replica_groups=rg))
        CC_MS = p.cc(lambda g: g.collective_compute(
            "AllGather", OP.bypass, ins=[cin_ms], outs=[cout_ms], replica_groups=rg))

        if P0_CUT == 3:
            for _tt in range(NTT):
                p.dma("gpsimd", out_t[_tt * 128:(_tt + 1) * 128, :],
                      x_in[_tt * 128:(_tt + 1) * 128, :])
            p.wait("gpsimd", "DG", p.cur("DG"))
            p.wait("sync", "DG", p.cur("DG"))
            raise _PhaseDone()

        dq_prev_v = 0
        for i in range(NKT):
            if i > 0:
                p.wait("sync", "V", dq_prev_v)
                p.wait("gpsimd", "V", dq_prev_v)
            vpk = p.dma("sync", pk_t[:], wpk_in[i * 128:(i + 1) * 128, :])
            vsc = p.dma("gpsimd", sc_f[:], wsc_in[i * 128:(i + 1) * 128, :], sem="DB")
            p.wait("vector", "DS", vpk)
            p.wait("vector", "DB", vsc)
            p.instr("vector", lambda v: v.tensor_scalar(
                out=hi_i[:], in0=pk_t[:], scalar1=4, scalar2=15,
                op0=OP.logical_shift_right, op1=OP.bitwise_and))
            p.instr("vector", lambda v: v.tensor_scalar(
                out=pk_t[:], in0=pk_t[:], scalar1=15, scalar2=None, op0=OP.bitwise_and))
            p.wait("vector", "V", p.cur("V"))
            p.instr("vector", lambda v: v.tensor_scalar(
                out=lo_f[:], in0=pk_t[:], scalar1=-8.0, scalar2=None, op0=OP.add))
            p.instr("vector", lambda v: v.tensor_scalar(
                out=hi_f[:], in0=hi_i[:], scalar1=-8.0, scalar2=None, op0=OP.add))
            p.wait("vector", "V", p.cur("V"))
            for r in range(16):
                fnc = (lambda v, r=r: v.tensor_copy(
                    sc_exp[:].rearrange("p (b r) -> p b r", r=16)[:, :, r], sc_f[:]))
                if r == 15:
                    p.instr("vector", fnc)
                else:
                    p.raw("vector", fnc)
            p.wait("vector", "V", p.cur("V"))
            wv = w_bf[:].rearrange("p (j two) -> p j two", two=2)
            p.instr("vector", lambda v: v.tensor_tensor(
                out=wv[:, :, 0], in0=lo_f[:], in1=sc_exp[:], op=OP.mult))
            vdq = p.instr("vector", lambda v: v.tensor_tensor(
                out=wv[:, :, 1], in0=hi_f[:], in1=sc_exp[:], op=OP.mult))
            dq_prev_v = vdq
            p.wait("tensor", "V", vdq)
            for kb in range(KTF // 4):
                hb = (kb % 2) * 1024
                for j in range(4):
                    kt = kb * 4 + j
                    fn = (lambda t, kt=kt, j=j, hb=hb: t.transpose(
                        ps_trb[:, hb + j * 128:hb + (j + 1) * 128], w_bf[:, kt * 128:(kt + 1) * 128], id16[:]))
                    if j == 3:
                        te = p.instr("tensor", fn)
                    else:
                        p.raw("tensor", fn)
                p.wait("scalar", "TE", te)
                se = p.instr("scalar", lambda s, kb=kb, hb=hb: s.activation(
                    out=wT_pc[:, kb * 512:(kb + 1) * 512], in_=ps_trb[:, hb:hb + 512], func=AF.Copy))
                if kb >= 1:
                    p.wait("tensor", "S", p.cur("S") - 1)
            p.wait("gpsimd", "S", p.cur("S"))
            p.dma("gpsimd",
                  cin_w.rearrange("(ktg p) do -> p ktg do", p=128)[:, :, i * 128:(i + 1) * 128],
                  wT_pc[:].rearrange("p (ktg t) -> p ktg t", ktg=KTF))
            p.wait("vector", "DG", p.cur("DG"))
        p.wait("gpsimd", "DG", p.cur("DG"))
        CC_W = p.cc(lambda g: g.collective_compute(
            "AllGather", OP.bypass, ins=[cin_w], outs=[cout_w], replica_groups=rg))

        if P0_CUT == 4:
            for _tt in range(NTT):
                p.dma("gpsimd", out_t[_tt * 128:(_tt + 1) * 128, :],
                      x_in[_tt * 128:(_tt + 1) * 128, :])
            p.wait("gpsimd", "DG", p.cur("DG"))
            p.wait("sync", "DG", p.cur("DG"))
            raise _PhaseDone()

        es0.close()
        bb_1 = S(es1, "bb_all", [128, NST * LC], F16)
        cb_1 = S(es1, "cb_all", [128, NST * LC], F16)
        bb_d2 = [bb_1[:], bb_1[:]]
        cb_d2 = [cb_1[:], cb_1[:]]
        dt_c = S(es1, "dt_c", [128, LC], F32)
        z_c = S(es1, "z_c", [128, LC], F32)
        zc_c = S(es1, "zc_c", [128, LC], F32)
        u_c = S(es1, "u_c", [128, LC], F16)
        xn_2 = S(es1, "xn_all", [128, 2 * NKT * LC], F16)
        xn_d2 = [xn_2[:, 0:NKT * LC], xn_2[:, NKT * LC:2 * NKT * LC]]
        da_t = S(es1, "da_d", [128, 3 * LC], F32)
        da_d = [da_t[:, 0:LC], da_t[:, LC:2 * LC], da_t[:, 2 * LC:3 * LC]]
        dbx_t = S(es1, "dbx_d", [128, 2 * LC], F16)
        dbx_d = [dbx_t[:, 0:LC], dbx_t[:, LC:2 * LC]]
        h_t = S(es1, "h_d", [128, 2 * LC], F16)
        h_d = [h_t[:, 0:LC], h_t[:, LC:2 * LC]]
        yv_t = S(es1, "yv_d", [128, 2 * LC], F16)
        yv_d = [yv_t[:, 0:LC], yv_t[:, LC:2 * LC]]
        y_sb = S(es1, "y_sb", [128, LC], F32)
        yT_sb = S(es1, "yT_sb", [128, LC], F16)
        w_r = S(es1, "w_r", [128, KTF * 512], F16)
        xnT_ft = S(es1, "xnT_f", [128, 2 * KTF * 128], F16)
        xnT_f = [xnT_ft[:, 0:KTF * 128], xnT_ft[:, KTF * 128:2 * KTF * 128]]
        g_sbt = S(es1, "g_sb", [128, 1024], F16)
        g_sb = [g_sbt[:, 0:512], g_sbt[:, 512:1024]]

        if P0_CUT == 5:
            for _tt in range(NTT):
                p.dma("gpsimd", out_t[_tt * 128:(_tt + 1) * 128, :],
                      x_in[_tt * 128:(_tt + 1) * 128, :])
            p.wait("gpsimd", "DG", p.cur("DG"))
            p.wait("sync", "DG", p.cur("DG"))
            raise _PhaseDone()

        p.wait("sync", "CC", CC_MS)
        if P0_CUT != 7:
            for r in range(NCORE):
                p.dma("sync", dtinT[:, r * TL:(r + 1) * TL], cout_ms[r * NMIX:r * NMIX + RNK, :],
                      sem="DD")
        V_DTIN = p.cur("DD")
        if P0_CUT in (6, 7):
            for _tt in range(NTT):
                p.dma("gpsimd", out_t[_tt * 128:(_tt + 1) * 128, :],
                      x_in[_tt * 128:(_tt + 1) * 128, :])
            p.wait("gpsimd", "DG", p.cur("DG"))
            p.wait("sync", "DG", p.cur("DG"))
            raise _PhaseDone()

        p.instr("vector", lambda v: v.memset(carry[:], 0.0))
        V_CARRY = p.cur("V")

        if DBG in ("xn", "xp"):
            p.wait("gpsimd", "DG", p.cur("DG"))
            p.wait("sync", "DG", p.cur("DG"))
            raise _PhaseDone()
        if DBG == "cinxf":
            p.wait("gpsimd", "CC", CC_W)
            p.dma("gpsimd",
                  out_t.flatten().rearrange("(p n) -> p n", p=128),
                  cin_xf.flatten().rearrange("(p n) -> p n", p=128))
            p.wait("gpsimd", "DG", p.cur("DG"))
            p.wait("sync", "DG", p.cur("DG"))
            raise _PhaseDone()
        if DBG == "dtins":
            p.wait("gpsimd", "DD", V_DTIN)
            p.dma("gpsimd", out_t.flatten()[0:128 * T].rearrange("(p t) -> p t", p=128),
                  dtinT[:])
            p.wait("gpsimd", "DG", p.cur("DG"))
            p.wait("sync", "DG", p.cur("DG"))
            raise _PhaseDone()
        if DBG == "dtin":
            p.wait("gpsimd", "DS", V_DTIN)
            p.dma("gpsimd",
                  out_t.flatten()[0:128 * T].rearrange("(p t) -> p t", p=128),
                  dtinT[:])
            p.wait("gpsimd", "DG", p.cur("DG"))
            p.wait("sync", "DG", p.cur("DG"))
            raise _PhaseDone()
        if PHASE_LIMIT == 0:
            for tt in range(NTT):
                p.dma("gpsimd", out_t[tt * 128:(tt + 1) * 128, :],
                      x_in[tt * 128:(tt + 1) * 128, :])
            p.wait("gpsimd", "DG", p.cur("DG"))
            p.wait("sync", "DG", p.cur("DG"))
            raise _PhaseDone()

        p.wait("sync", "CC", CC_XN)
        s_ds0 = p.sem("DX")
        if STAGE_ON:
            p.counts["DX"] += 16 * NCORE
            def stage_xn(e, _s=s_ds0):
                pid = pid_cache["sy"]
                for ch in range(NCORE):
                    if ch > 0:
                        e.wait_ge(_s, 16 * ch)
                    e.dma_start(out=xn_loc[ch], in_=cout_xa[ch, bass.ds(pid, 1)]).then_inc(_s, 16)
            p.ops["sync"].append(stage_xn)
        V_XNSTAGE = p.cur("DX")

        ffn = {"r": 0, "tt": 0, "w_loaded": -1, "done": False,
               "bank_evac": [0, 0],
               "gsb_store": [0, 0],
               "xnf_mm": [0, 0],
               "w_v": 0}

        def emit_ffn_step():
            if ffn["done"] or not FFN_ON:
                return
            r, tt = ffn["r"], ffn["tt"]
            if tt == 0 and ffn["w_loaded"] != r:
                p.wait("sync", "CC", CC_W)
                if r > 0:
                    p.wait("sync", "TE", p.cur("TE"))
                p.dma("sync", w_r[:].rearrange("p (ktg do) -> p ktg do", ktg=KTF),
                      cout_w[r * D:(r + 1) * D, :].rearrange("(ktg p) do -> p ktg do", p=128),
                      sem="DWR")
                ffn["w_v"] = p.cur("DWR")
                ffn["w_loaded"] = r
            buf = tt % 2
            p.wait("sync", "TE", ffn["xnf_mm"][buf])
            if DBG == "xnf" and r == 0 and tt == 0:
                sv_ = p.instr("vector", lambda v, buf=buf: v.memset(xnT_f[buf], 7.0))
                p.wait("sync", "V", sv_)
            p.dma("sync", xnT_f[buf], cin_xg[tt], sem="DW0" if buf == 0 else "DW1")
            v_xn = p.cur("DW0" if buf == 0 else "DW1")
            if DBG == "wdq" and r == 0 and tt == 0:
                p.wait("gpsimd", "DWR", ffn["w_v"])
                p.dma("gpsimd",
                      out_t.flatten()[0:128 * KTF * 512].rearrange("(p n) -> p n", p=128),
                      w_r[:])
            p.wait("tensor", "DWR", ffn["w_v"])
            p.wait("tensor", "DW0" if buf == 0 else "DW1", v_xn)
            if DBG == "xnf" and r == 0 and tt == 0:
                p.wait("gpsimd", "DS", v_xn)
                p.dma("gpsimd",
                      out_t.flatten()[0:128 * KTF * 128].rearrange("(p n) -> p n", p=128),
                      xnT_f[buf])
            bank = ps_ffn[:, buf * 512:(buf + 1) * 512]
            p.wait("tensor", "S", ffn["bank_evac"][buf])
            for kt in range(KTF):
                fn = (lambda t, kt=kt, buf=buf, bank=bank: t.matmul(
                    bank, xnT_f[buf][:, kt * 128:(kt + 1) * 128],
                    w_r[:, kt * 512:(kt + 1) * 512],
                    start=(kt == 0), stop=(kt == KTF - 1)))
                if kt == KTF - 1:
                    te = p.instr("tensor", fn)
                else:
                    p.raw("tensor", fn)
            ffn["xnf_mm"][buf] = te
            p.wait("scalar", "TE", te)
            p.wait("scalar", "DG0" if buf == 0 else "DG1", ffn["gsb_store"][buf])
            se = p.instr("scalar", lambda s, buf=buf, bank=bank: s.activation(
                out=g_sb[buf], in_=bank, func=AF.Silu))
            ffn["bank_evac"][buf] = se
            p.wait("gpsimd", "S", se)
            p.dma("gpsimd", g_buf[tt * 128:(tt + 1) * 128, r * 512:(r + 1) * 512], g_sb[buf],
                  sem="DG0" if buf == 0 else "DG1")
            ffn["gsb_store"][buf] = p.cur("DG0" if buf == 0 else "DG1")
            ffn["tt"] += 1
            if ffn["tt"] == NTT:
                ffn["tt"] = 0
                ffn["r"] += 1
                if ffn["r"] == NCORE:
                    ffn["done"] = True

        p.wait("vector", "V", V_CARRY)
        scan_sem = {}
        nctr = [0]
        ymm_sem = [0, 0]
        yfin_state = {"vv_y": 0, "te_tr": 0, "dg_y": 0, "da_last": 0}
        def load_bc(chl):
            p.dma("gpsimd", bb_d2[chl % 2][:].rearrange("p (n t) -> p n t", n=NST),
                  cout_ms[chl * NMIX + RNK:chl * NMIX + RNK + NST, :].partition_broadcast(128),
                  sem="DB")
            p.dma("gpsimd", cb_d2[chl % 2][:].rearrange("p (n t) -> p n t", n=NST),
                  cout_ms[chl * NMIX + RNK + NST:chl * NMIX + NMIX, :].partition_broadcast(128),
                  sem="DB")
            return p.cur("DB")
        def load_xn(chl):
            for i in range(NTT):
                p.dma("scalar",
                      xn_d2[chl % 2][:].rearrange("p (ktl i t) -> p ktl i t",
                                                  ktl=NKT, i=NTT)[:, :, i, :],
                      xn_loc[chl, i].rearrange("(ktl p) t -> p ktl t", p=128))
            return p.cur("DA")
        v_bc_of = {}
        v_xn_of = {}
        v_chend = [0, 0]
        for ch in range(NCH_LIM):
            if ch == 0:
                p.wait("gpsimd", "CC", CC_MS)
                p.wait("gpsimd", "CC", CC_W)
                p.wait("scalar", "CC", CC_XN)
                p.wait("scalar", "CC", CC_W)
                p.wait("scalar", "DX", V_XNSTAGE)
                v_xn_of[0] = load_xn(0)
            else:
                p.wait("gpsimd", "V", p.cur("V"))
            v_bc_of[ch] = load_bc(ch)
            if ch + 1 < NCH_LIM:
                p.wait("scalar", "V", v_chend[(ch + 1) % 2])
                v_xn_of[ch + 1] = load_xn(ch + 1)
            bb_all = bb_d2[ch % 2]
            cb_all = cb_d2[ch % 2]
            xn_all = xn_d2[ch % 2]
            V_BC = v_bc_of[ch]
            V_XNC = v_xn_of[ch]
            for kt in range(NKT):
                p.wait("tensor", "DD", V_DTIN)
                for q in range(2):
                    te = p.instr("tensor", lambda t, kt=kt, ch=ch, q=q: t.matmul(
                        ps_dt[:, 0:512], wdtT[:, kt * 128:(kt + 1) * 128],
                        dtinT[:, ch * LC + q * 512:ch * LC + (q + 1) * 512],
                        start=True, stop=True))
                    p.wait("vector", "TE", te)
                    if q == 0:
                        p.wait("vector", "S", yfin_state["da_last"])
                    vv = p.instr("vector", lambda v, q=q, kt=kt: v.tensor_scalar(
                        out=z_c[:, q * 512:(q + 1) * 512], in0=ps_dt[:, 0:512],
                        scalar1=dtb_par[:, kt:kt + 1], scalar2=None, op0=OP.add))
                    if (DBG or "").startswith("zone") and q == 0 and                             (ch, kt) == tuple(int(v) for v in (DBG.split(":")[1:3] or [0, 0])):
                        p.wait("gpsimd", "V", p.cur("V"))
                        p.dma("gpsimd", out_t.flatten()[0:65536].rearrange("(p t) -> p t", p=128), z_c[:, 0:512])
                        p.dma("gpsimd", out_t.flatten()[65536:131072].rearrange("(p t) -> p t", p=128), dtinT[:, ch * LC:ch * LC + 512])
                        p.dma("gpsimd", out_t.flatten()[131072:147456].rearrange("(p t) -> p t", p=128), wdtT[:, kt * 128:(kt + 1) * 128])
                        p.dma("gpsimd", out_t.flatten()[147456:147584].rearrange("(p t) -> p t", p=128), dtb_par[:, kt:kt + 1])
                        p.wait("gpsimd", "DG", p.cur("DG"))
                        p.wait("sync", "DG", p.cur("DG"))
                        raise _PhaseDone()
                    p.wait("tensor", "V", vv)
                p.instr("vector", lambda v: v.tensor_scalar(
                    out=zc_c[:], in0=z_c[:], scalar1=20.0, scalar2=None, op0=OP.min))
                p.wait("scalar", "V", p.cur("V"))
                p.instr("scalar", lambda s: s.activation(out=zc_c[:], in_=zc_c[:], func=AF.Exp))
                se = p.instr("scalar", lambda s: s.activation(
                    out=zc_c[:], in_=zc_c[:], func=AF.Ln, bias=1.0))
                p.wait("vector", "S", se)
                p.instr("vector", lambda v: v.tensor_tensor(
                    out=dt_c[:], in0=zc_c[:], in1=z_c[:], op=OP.max))
                if (DBG or "").startswith("dtone") and                         (ch, kt) == tuple(int(v) for v in (DBG.split(":")[1:3] or [0, 0])):
                    p.wait("gpsimd", "V", p.cur("V"))
                    p.dma("gpsimd", out_t.flatten()[0:131072].rearrange("(p t) -> p t", p=128), dt_c[:])
                    p.dma("gpsimd", out_t.flatten()[131072:262144].rearrange("(p t) -> p t", p=128), z_c[:])
                    p.wait("gpsimd", "DG", p.cur("DG"))
                    p.wait("sync", "DG", p.cur("DG"))
                    raise _PhaseDone()
                if DBG == "zraw":
                    p.wait("gpsimd", "V", p.cur("V"))
                    offz = (ch * NKT + kt) * 128 * 1024
                    p.dma("gpsimd",
                          out_t.flatten()[offz:offz + 131072].rearrange("(p t) -> p t", p=128),
                          z_c[:])
                    p.wait("vector", "DG", p.cur("DG"))
                if DBG == "dt":
                    p.wait("gpsimd", "V", p.cur("V"))
                    off = (ch * NKT + kt) * 128 * 1024
                    p.dma("gpsimd",
                          out_t.flatten()[off:off + 131072].rearrange("(p t) -> p t", p=128),
                          dt_c[:])
                    p.wait("vector", "DG", p.cur("DG"))
                p.wait("vector", "DA", V_XNC)
                p.instr("vector", lambda v, kt=kt, xn_all=xn_all: v.tensor_tensor(
                    out=u_c[:], in0=dt_c[:], in1=xn_all[:, kt * LC:(kt + 1) * LC],
                    op=OP.mult))
                V_U = p.cur("V")
                p.wait("scalar", "V", V_U)
                p.wait("vector", "DB", V_BC)
                for n in range(NST):
                    par = n % 2
                    apar = nctr[0] % 3
                    nctr[0] += 1
                    if apar in scan_sem:
                        p.wait("scalar", "V", scan_sem[apar])
                    se_da = p.instr("scalar", lambda s, kt=kt, n=n, apar=apar: s.activation(
                        out=da_d[apar], in_=dt_c[:], func=AF.Exp,
                        scale=a_par[:, kt * NST + n:kt * NST + n + 1]))
                    vv_dbx = p.instr("vector", lambda v, n=n, par=par, bb_all=bb_all: v.tensor_tensor(
                        out=dbx_d[par], in0=u_c[:], in1=bb_all[:, n * LC:(n + 1) * LC],
                        op=OP.mult))
                    p.wait("vector", "S", se_da)
                    vv_scan = p.instr("vector", lambda v, kt=kt, n=n, par=par, apar=apar: v.tensor_tensor_scan(
                        out=h_d[par], data0=da_d[apar], data1=dbx_d[par],
                        initial=carry[:, kt * NST + n:kt * NST + n + 1],
                        op0=OP.mult, op1=OP.add))
                    scan_sem[apar] = vv_scan
                    p.wait("scalar", "V", vv_scan)
                    p.instr("scalar", lambda s, kt=kt, n=n, par=par: s.activation(
                        out=carry[:, kt * NST + n:kt * NST + n + 1],
                        in_=h_d[par][:, LC - 1:LC], func=AF.Copy))
                    p.wait("vector", "TE", ymm_sem[par])
                    vv_yv = p.instr("vector", lambda v, n=n, par=par, cb_all=cb_all: v.tensor_tensor(
                        out=yv_d[par], in0=h_d[par], in1=cb_all[:, n * LC:(n + 1) * LC],
                        op=OP.mult))
                    p.wait("tensor", "V", vv_yv)
                    if n == 0:
                        p.wait("tensor", "V", yfin_state["vv_y"])
                    for q in range(2):
                        fn = (lambda t, q=q, n=n, par=par: t.matmul(
                            ps_y[:, q * 512:(q + 1) * 512], id16[:],
                            yv_d[par][:, q * 512:(q + 1) * 512],
                            start=(n == 0), stop=(n == NST - 1)))
                        if q == 1:
                            te_y = p.instr("tensor", fn)
                        else:
                            p.raw("tensor", fn)
                    ymm_sem[par] = te_y
                yfin_state["da_last"] = p.cur("S")
                p.wait("vector", "TE", te_y)
                p.wait("vector", "TE", yfin_state["te_tr"])
                p.instr("vector", lambda v, kt=kt, xn_all=xn_all: v.tensor_scalar(
                    out=y_sb[:], in0=xn_all[:, kt * LC:(kt + 1) * LC],
                    scalar1=d_par[:, kt:kt + 1], scalar2=None, op0=OP.mult))
                vv_y = p.instr("vector", lambda v: v.tensor_tensor(
                    out=y_sb[:], in0=y_sb[:], in1=ps_y[:], op=OP.add))
                yfin_state["vv_y"] = vv_y
                if (DBG or "").startswith("yone") and                         (ch, kt) == tuple(int(v) for v in (DBG.split(":")[1:3] or [0, 0])):
                    p.wait("gpsimd", "V", p.cur("V"))
                    p.dma("gpsimd", out_t.flatten()[0:131072].rearrange("(p t) -> p t", p=128), y_sb[:])
                    p.dma("gpsimd", out_t.flatten()[131072:262144].rearrange("(p t) -> p t", p=128), dt_c[:])
                    p.dma("gpsimd", out_t.flatten()[262144:262144 + 16384].rearrange("(p t) -> p t", p=128),
                          bb_all[:, 0:128])
                    p.wait("gpsimd", "DG", p.cur("DG"))
                    p.wait("sync", "DG", p.cur("DG"))
                    raise _PhaseDone()
                p.wait("tensor", "V", vv_y)
                for half in range(2):
                    if half == 1:
                        p.wait("tensor", "S", p.cur("S"))
                    for j in range(4):
                        i = half * 4 + j
                        fn = (lambda t, i=i, j=j: t.transpose(
                            ps_trf[:, j * 128:(j + 1) * 128],
                            y_sb[:, i * 128:(i + 1) * 128], id32[:]))
                        if j == 3:
                            te = p.instr("tensor", fn)
                        else:
                            p.raw("tensor", fn)
                    p.wait("scalar", "TE", te)
                    if half == 0:
                        p.wait("scalar", "DY", yfin_state["dg_y"])
                    se = p.instr("scalar", lambda s, half=half: s.activation(
                        out=yT_sb[:, half * 512:(half + 1) * 512], in_=ps_trf[:, 0:512],
                        func=AF.Copy))
                yfin_state["te_tr"] = te
                p.wait("gpsimd", "S", se)
                p.dma("gpsimd", cin_y[ch, :, kt].rearrange("i p q -> p i q"),
                      yT_sb[:].rearrange("p (i q) -> p i q", i=NTT), sem="DY")
                yfin_state["dg_y"] = p.cur("DY")
                if (DBG or "").startswith("ycin") and                         (ch, kt) == tuple(int(v) for v in (DBG.split(":")[1:3] or [0, 0])):
                    p.wait("gpsimd", "DG", p.cur("DG"))
                    for i_ in range(NTT):
                        p.dma("gpsimd",
                              out_t.flatten()[i_ * 16384:(i_ + 1) * 16384]
                              .rearrange("(p d) -> p d", p=128),
                              cin_y[ch, i_, kt])
                    p.dma("gpsimd", out_t.flatten()[131072:262144].rearrange("(p t) -> p t", p=128),
                          yT_sb[:])
                    p.wait("gpsimd", "DG", p.cur("DG"))
                    p.wait("sync", "DG", p.cur("DG"))
                    raise _PhaseDone()
                if (DBG or "").startswith("yat") and                         (ch, kt) == tuple(int(v) for v in DBG.split(":")[1:3]):
                    p.wait("gpsimd", "DY", p.cur("DY"))
                    for i_ in range(NTT):
                        for k_ in range(NKT):
                            off_ = (i_ * NKT + k_) * 16384
                            p.dma("gpsimd", out_t.flatten()[off_:off_ + 16384]
                                  .rearrange("(p d) -> p d", p=128),
                                  cin_y[0, i_, k_])
                    p.wait("gpsimd", "DG", p.cur("DG"))
                    p.wait("sync", "DG", p.cur("DG"))
                    raise _PhaseDone()
                p.wait("tensor", "S", p.cur("S"))
                emit_ffn_step()
                emit_ffn_step()
                if kt == NKT - 1:
                    v_chend[ch % 2] = p.cur("V")
                    p.wait("gpsimd", "DY", p.cur("DY"))
                    cc_y_last = p.cc(lambda g, ch=ch: g.collective_compute(
                        "AllGather", OP.bypass, ins=[cin_y[ch]], outs=[cout_ya[ch]],
                        replica_groups=rg))
        while FFN_ON and not ffn["done"]:
            emit_ffn_step()
        p.wait("gpsimd", "DG0", p.cur("DG0"))
        p.wait("gpsimd", "DG1", p.cur("DG1"))
        p.wait("gpsimd", "DG", p.cur("DG"))
        CC_Y = cc_y_last

        es1.close()
        xt_c2 = S(es2, "xt2", [128, 2 * D], F32)
        g_t2 = S(es2, "g_t", [128, 2 * D], F16)
        sT_in2 = S(es2, "sT_in", [128, 2 * D], F16)
        o_t2 = S(es2, "o_t", [128, 2 * D], F32)
        xt_c = xt_c2[:, 0:D]
        

        if DBG in ("wdq", "xnf"):
            p.wait("gpsimd", "DG", p.cur("DG"))
            p.wait("sync", "DG", p.cur("DG"))
            raise _PhaseDone()
        if DBG in ("dt", "zraw"):
            p.wait("gpsimd", "DG", p.cur("DG"))
            p.wait("sync", "DG", p.cur("DG"))
            raise _PhaseDone()
        if DBG == "g":
            for _tt in range(NTT):
                p.dma("gpsimd", out_t[_tt * 128:(_tt + 1) * 128, :],
                      g_buf[_tt * 128:(_tt + 1) * 128, :])
            p.wait("gpsimd", "DG", p.cur("DG"))
            p.wait("sync", "DG", p.cur("DG"))
            raise _PhaseDone()
        if PHASE_LIMIT == 1:
            for tt in range(NTT):
                p.dma("gpsimd", out_t[tt * 128:(tt + 1) * 128, :],
                      x_in[tt * 128:(tt + 1) * 128, :])
            p.wait("gpsimd", "CC", CC_Y)
            p.wait("gpsimd", "DG", p.cur("DG"))
            p.wait("sync", "DG", p.cur("DG"))
            raise _PhaseDone()

        if DBG == "ycfin":
            p.wait("gpsimd", "CC", CC_Y)
            p.dma("gpsimd", out_t.flatten()[0:NTT * NKT * 128 * 128]
                  .rearrange("(a b) -> a b", b=128),
                  cin_y[0].rearrange("i k p d -> (i k p) d"))
            p.wait("gpsimd", "DG", p.cur("DG"))
            p.wait("sync", "DG", p.cur("DG"))
            raise _PhaseDone()

        p.wait("sync", "CC", CC_Y)
        p.wait("sync", "G", p.cur("G"))
        p.wait("scalar", "CC", CC_Y)
        s_ds1 = p.sem("DS")
        ds_base_y = p.counts["DS"]
        p.counts["DS"] += 16
        def stage_y_lo(e, _s=s_ds1, _b=ds_base_y):
            pid = pid_cache["sy"]
            if _b > 0:
                e.wait_ge(_s, _b)
            e.dma_start(out=s_loc[0:4], in_=cout_ya[bass.ds(pid, 1), 0:4]).then_inc(_s, 16)
        p.ops["sync"].append(stage_y_lo)
        s_da1 = p.sem("DA")
        da_base_y = p.counts["DA"]
        p.counts["DA"] += 16
        def stage_y_hi(e, _s=s_da1, _b=da_base_y):
            pid = pid_cache["sc"]
            if _b > 0:
                e.wait_ge(_s, _b)
            e.dma_start(out=s_loc[4:8], in_=cout_ya[bass.ds(pid, 1), 4:8]).then_inc(_s, 16)
        p.ops["scalar"].append(stage_y_hi)
        vv_hist = [0, 0]
        dg_hist = [0, 0]
        ld_hist = []
        for tt in range(NTT):
            buf = tt % 2
            xt_b = xt_c2[:, buf * D:(buf + 1) * D]
            g_b = g_t2[:, buf * D:(buf + 1) * D]
            sT_b = sT_in2[:, buf * D:(buf + 1) * D]
            o_b = o_t2[:, buf * D:(buf + 1) * D]
            p.wait("sync", "V", vv_hist[buf])
            p.wait("scalar", "V", vv_hist[buf])
            p.wait("gpsimd", "V", vv_hist[buf])
            for r in range(4):
                p.dma("sync",
                      sT_b[:, r * 512:(r + 1) * 512].rearrange("p (ktl t) -> p ktl t", ktl=NKT),
                      s_loc[r, tt].rearrange("ktl p t -> p ktl t"))
            vs_ds = p.cur("DS")
            for r in range(4, NCORE):
                p.dma("scalar",
                      sT_b[:, r * 512:(r + 1) * 512].rearrange("p (ktl t) -> p ktl t", ktl=NKT),
                      s_loc[r, tt].rearrange("ktl p t -> p ktl t"))
            vx_da = p.dma("scalar", xt_b, x_in[tt * 128:(tt + 1) * 128, :])
            vg_dg = p.dma("gpsimd", g_b, g_buf[tt * 128:(tt + 1) * 128, :], sem="DG2")
            p.wait("vector", "DS", vs_ds)
            p.wait("vector", "DA", vx_da)
            p.wait("vector", "DG2", vg_dg)
            p.wait("vector", "DG", dg_hist[buf])
            if DBG == "s":
                vv = p.instr("vector", lambda v, sT_b=sT_b, o_b=o_b: v.tensor_copy(o_b, sT_b))
            else:
                p.instr("vector", lambda v, sT_b=sT_b, g_b=g_b, o_b=o_b: v.tensor_tensor(
                    out=o_b, in0=sT_b, in1=g_b, op=OP.mult))
                vv = p.instr("vector", lambda v, o_b=o_b, xt_b=xt_b: v.tensor_tensor(
                    out=o_b, in0=o_b, in1=xt_b, op=OP.add))
            vv_hist[buf] = vv
            p.wait("gpsimd", "V", vv)
            p.dma("gpsimd", out_t[tt * 128:(tt + 1) * 128, :], o_b)
            dg_hist[buf] = p.cur("DG")
        p.wait("gpsimd", "DG", p.cur("DG"))
        p.wait("sync", "DG", p.cur("DG"))

        es2.close()
     except _PhaseDone:
        es2.close()
        es1.close()
        es0.close()

     if True:
        @block.sync
        def _(sync):
            for fn in p.ops["sync"]:
                fn(nc.sync)

        @block.vector
        def _(vector):
            for fn in p.ops["vector"]:
                fn(nc.vector)

        @block.scalar
        def _(scalar):
            for fn in p.ops["scalar"]:
                fn(nc.scalar)

        @block.tensor
        def _(tensor):
            for fn in p.ops["tensor"]:
                fn(nc.tensor)

        @block.gpsimd
        def _(gpsimd):
            for fn in p.ops["gpsimd"]:
                fn(nc.gpsimd)

    return nc


_NC_CACHE = None


def kernel(L0_attn_x, L0_attn_A_log, L0_attn_D, L0_attn_x_proj_w,
           L0_attn_dt_proj_w, L0_attn_dt_proj_b, L0_ffn_weight_packed,
           L0_ffn_scales):
    global _NC_CACHE
    x = np.ascontiguousarray(np.asarray(L0_attn_x, dtype=np.float32).reshape(T, D))
    alog = np.ascontiguousarray(np.asarray(L0_attn_A_log, dtype=np.float32))
    dvec = np.ascontiguousarray(np.asarray(L0_attn_D, dtype=np.float32))
    wxp = np.ascontiguousarray(np.asarray(L0_attn_x_proj_w, dtype=np.float32))
    wdt = np.ascontiguousarray(np.asarray(L0_attn_dt_proj_w, dtype=np.float32))
    dtb = np.ascontiguousarray(np.asarray(L0_attn_dt_proj_b, dtype=np.float32))
    wpk = np.ascontiguousarray(np.asarray(L0_ffn_weight_packed, dtype=np.int32))
    wsc = np.ascontiguousarray(np.asarray(L0_ffn_scales, dtype=np.float16))

    if _NC_CACHE is None:
        _NC_CACHE = build_program()
    nc = _NC_CACHE

    in_maps = []
    for c in range(NCORE):
        in_maps.append({
            "x_in": x[c * TL:(c + 1) * TL],
            "alog_in": alog[c * DS:(c + 1) * DS],
            "dvec_in": dvec[c * DS:(c + 1) * DS],
            "wxp_in": wxp,
            "wdt_in": wdt[c * DS:(c + 1) * DS],
            "dtb_in": dtb[c * DS:(c + 1) * DS],
            "wpk_in": wpk[c * DS:(c + 1) * DS],
            "wsc_in": wsc[c * DS:(c + 1) * DS],
        })
    res = run_bass_kernel_spmd(nc, in_maps, core_ids=list(range(NCORE)))
    out = np.concatenate([res.results[c]["out_t"] for c in range(NCORE)], axis=0)
    return out.reshape(1, T, D)


# version 12 (final)
# speedup vs baseline: 1.1858x; 1.1858x over previous
import numpy as np
import ml_dtypes
import concourse.bass as bass
import concourse.mybir as mybir
from concourse.alu_op_type import AluOpType
from concourse.bass_utils import run_bass_kernel_spmd

F32 = mybir.dt.float32
BF16 = mybir.dt.bfloat16
I32 = mybir.dt.int32
F16 = mybir.dt.float16
AF = mybir.ActivationFunctionType
OP = AluOpType
AX = mybir.AxisListType

NCORE = 8
T = 8192
TL = T // NCORE
D = 4096
DS = D // NCORE
NKT = DS // 128
KTF = D // 128
NST = 16
RNK = 128
NMIX = RNK + 2 * NST
LC = 1024
NCH = T // LC
NTT = TL // 128
EPS = 1e-5

PHASE_LIMIT = 2
FFN_ON = True
STAGE_ON = True
NCH_LIM = NCH
P0_CUT = 99
DBG = None


class Plan:

    ENGS = ("sync", "vector", "scalar", "tensor", "gpsimd")

    def __init__(self, nc):
        self.nc = nc
        self.ops = {k: [] for k in self.ENGS}
        self.sems = {}
        self.counts = {}

    def sem(self, name):
        if name not in self.sems:
            self.sems[name] = self.nc.semaphore(name).__enter__()
            self.counts[name] = 0
        return self.sems[name]

    def dma(self, eng, out, in_, sem=None, **kw):
        sem_name = sem or {"sync": "DS", "scalar": "DA", "gpsimd": "DG", "vector": "DV"}[eng]
        s = self.sem(sem_name)
        prev = self.counts[sem_name]
        self.counts[sem_name] += 16
        def fn(e, _s=s, _o=out, _i=in_, _kw=kw, _p=prev):
            if _p > 0:
                e.wait_ge(_s, _p)
            e.dma_start(out=_o, in_=_i, **_kw).then_inc(_s, 16)
        self.ops[eng].append(fn)
        return self.counts[sem_name]

    def instr(self, eng, builder):
        sem_name = {"vector": "V", "scalar": "S", "tensor": "TE", "gpsimd": "G"}[eng]
        s = self.sem(sem_name)
        self.counts[sem_name] += 1
        def fn(e, _s=s, _b=builder):
            _b(e).then_inc(_s, 1)
        self.ops[eng].append(fn)
        return self.counts[sem_name]

    def raw(self, eng, builder):
        self.ops[eng].append(lambda e, _b=builder: _b(e))

    def cc(self, builder):
        s = self.sem("CC")
        self.counts["CC"] += 1
        def fn(e, _s=s, _b=builder):
            _b(e).then_inc(_s, 1)
        self.ops["gpsimd"].append(fn)
        return self.counts["CC"]

    def wait(self, eng, sem_name, value):
        s = self.sem(sem_name)
        if value <= 0:
            return
        def fn(e, _s=s, _v=value):
            e.wait_ge(_s, _v)
        self.ops[eng].append(fn)

    def cur(self, sem_name):
        self.sem(sem_name)
        return self.counts[sem_name]


class _PhaseDone(Exception):
    pass


def build_program():
    nc = bass.Bass("TRN2", target_bir_lowering=False)
    nc.detect_race_conditions = False

    x_in = nc.dram_tensor("x_in", [TL, D], F32, kind="ExternalInput").ap()
    alog_in = nc.dram_tensor("alog_in", [DS, NST], F32, kind="ExternalInput").ap()
    dvec_in = nc.dram_tensor("dvec_in", [DS], F32, kind="ExternalInput").ap()
    wxp_in = nc.dram_tensor("wxp_in", [NMIX, D], F32, kind="ExternalInput").ap()
    wdt_in = nc.dram_tensor("wdt_in", [DS, RNK], F32, kind="ExternalInput").ap()
    dtb_in = nc.dram_tensor("dtb_in", [DS], F32, kind="ExternalInput").ap()
    wpk_in = nc.dram_tensor("wpk_in", [DS, D // 2], I32, kind="ExternalInput").ap()
    wsc_in = nc.dram_tensor("wsc_in", [DS, D // 32], F16, kind="ExternalInput").ap()
    out_t = nc.dram_tensor("out_t", [TL, D], F32, kind="ExternalOutput").ap()

    cin_xf = nc.dram_tensor("cin_xf", [NCORE, NTT, NKT * 128, 128], F16, kind="Internal").ap()
    cout_xa = nc.dram_tensor("cout_xa", [NCORE, NCORE, NTT, NKT * 128, 128], F16,
                             kind="Internal", addr_space="Shared").ap()
    xn_loc = nc.dram_tensor("xn_loc", [NCORE, NTT, NKT * 128, 128], F16, kind="Internal").ap()
    cin_xg = nc.dram_tensor("cin_xg", [NTT, 128, KTF * 128], F16, kind="Internal").ap()
    cin_ms = nc.dram_tensor("cin_ms", [NMIX, TL], F16, kind="Internal").ap()
    cout_ms = nc.dram_tensor("cout_ms", [NCORE * NMIX, TL], F16, kind="Internal",
                             addr_space="Shared").ap()
    cin_w = nc.dram_tensor("cin_w", [D, DS], F16, kind="Internal").ap()
    cout_w = nc.dram_tensor("cout_w", [NCORE * D, DS], F16, kind="Internal",
                            addr_space="Shared").ap()
    cin_y = nc.dram_tensor("cin_y", [NCH, NTT, NKT, 128, 128], F16, kind="Internal").ap()
    cout_ya = nc.dram_tensor("cout_ya", [NCH, NCORE, NTT, NKT, 128, 128], F16, kind="Internal",
                             addr_space="Shared").ap()
    s_loc = nc.dram_tensor("s_loc", [NCORE, NTT, NKT, 128, 128], F16, kind="Internal").ap()
    g_buf = nc.dram_tensor("g_buf", [TL, D], F16, kind="Internal").ap()

    p = Plan(nc)
    rg = [list(range(NCORE))]
    pid_cache = {}
    def grab_pid_sync(e):
        pid_cache["sy"] = nc.partition_id([mybir.EngineType.SP])
    def grab_pid_sc(e):
        pid_cache["sc"] = nc.partition_id([mybir.EngineType.Activation])
    p.ops["sync"].append(grab_pid_sync)
    p.ops["scalar"].append(grab_pid_sc)

    import contextlib
    es = contextlib.ExitStack()
    es0 = contextlib.ExitStack()
    es1 = contextlib.ExitStack()
    es2 = contextlib.ExitStack()
    def S(st, *a):
        return st.enter_context(nc.sbuf_tensor(*a))
    def PS(*a):
        return es.enter_context(nc.psum_tensor(*a))
    id16 = S(es, "id16", [128, 128], F16)
    id32 = S(es, "id32", [128, 128], F32)
    a_par = S(es, "a_par", [128, NKT * NST], F32)
    d_par = S(es, "d_par", [128, NKT], F32)
    dtb_par = S(es, "dtb_par", [128, NKT], F32)
    wdtT = S(es, "wdtT", [128, NKT * 128], F16)
    dtinT = S(es, "dtinT", [128, T], F16)
    carry = S(es, "carry", [128, NKT * NST], F32)
    st_a = S(es, "st_a", [128, 16], F32)
    eps_t = S(es, "eps_t", [128, 1], F32)
    xp_t = S(es, "xp_t", [128, NMIX], F32)
    wxpT = S(es0, "wxpT", [128, KTF * NMIX], F16)
    bT_loc = S(es0, "bT_loc", [NST, TL], F16)
    cT_loc = S(es0, "cT_loc", [NST, TL], F16)
    dtinT_loc = S(es0, "dtinT_loc", [128, TL], F16)
    iota_i = S(es0, "iota_i", [128, 128], I32)
    xt = S(es0, "xt", [128, D], F32)
    xnt = S(es0, "xnt", [128, D], F16)
    sq_scr = S(es0, "sq_scr", [128, D], F32)
    xnT_tt = S(es0, "xnT_tt", [128, KTF * 128], F16)
    pk_t = S(es0, "pk_t", [128, D // 2], I32)
    hi_i = S(es0, "hi_i", [128, D // 2], I32)
    lo_f = S(es0, "lo_f", [128, D // 2], F32)
    hi_f = S(es0, "hi_f", [128, D // 2], F32)
    sc_f = S(es0, "sc_f", [128, D // 32], F32)
    sc_exp = S(es0, "sc_exp", [128, D // 2], F32)
    w_bf = S(es0, "w_bf", [128, D], F16)
    wT_pc = S(es0, "wT_pc", [128, KTF * 128], F16)
    ps_ffn = PS("ps_ffn", [128, 1024], F32)
    ps_y = PS("ps_y", [128, LC], F32)
    ps_dt = PS("ps_dt", [128, 512], F32)
    ps_trb = PS("ps_trb", [128, 2048], F16)
    ps_trf = PS("ps_trf", [128, 512], F32)
    block = es.enter_context(nc.Block())
    with es:
     try:

        p.instr("gpsimd", lambda g: g.iota(iota_i[:], pattern=[[1, 128]],
                                           base=0, channel_multiplier=-1))
        p.wait("vector", "G", p.cur("G"))
        vq = p.instr("vector", lambda v: v.tensor_scalar(
            out=id32[:], in0=iota_i[:], scalar1=0, scalar2=None, op0=OP.is_equal))
        p.wait("vector", "V", vq)
        p.instr("vector", lambda v: v.tensor_copy(id16[:], id32[:]))
        p.wait("vector", "V", p.cur("V"))
        p.instr("vector", lambda v: v.memset(eps_t[:], EPS))
        V_ID = p.cur("V")

        p.dma("sync", a_par[:].rearrange("p (kt n) -> p kt n", n=NST), alog_in.rearrange("(kt p) n -> p kt n", p=128))
        p.dma("sync", d_par[:], dvec_in.rearrange("(kt p) -> p kt", p=128), allow_slow_non_contiguous=True)
        p.dma("sync", dtb_par[:], dtb_in.rearrange("(kt p) -> p kt", p=128), allow_slow_non_contiguous=True)
        p.wait("scalar", "DS", p.cur("DS"))
        p.instr("scalar", lambda s: s.activation(out=a_par[:], in_=a_par[:], func=AF.Exp))
        p.wait("vector", "S", p.cur("S"))
        p.instr("vector", lambda v: v.tensor_scalar(
            out=a_par[:], in0=a_par[:], scalar1=-1.0, scalar2=None, op0=OP.mult))
        V_APAR = p.cur("V")

        with nc.sbuf_tensor("wxp_nat", [128, D], F32) as wxp_nat, \
             nc.sbuf_tensor("wxp_nb", [128, D], F16) as wxp_nb, \
             nc.sbuf_tensor("wxp_nb2", [32, D], F16) as wxp_nb2, \
             nc.sbuf_tensor("wdt_nat", [128, NKT * RNK], F32) as wdt_nat, \
             nc.sbuf_tensor("wdt_nb", [128, NKT * RNK], F16) as wdt_nb:
            v1 = p.dma("sync", wxp_nat[:], wxp_in[0:128, :])
            p.wait("vector", "DS", v1)
            p.instr("vector", lambda v: v.tensor_copy(wxp_nb[:], wxp_nat[:]))
            p.wait("sync", "V", p.cur("V"))
            v2 = p.dma("sync", wxp_nat[0:32, :], wxp_in[128:160, :])
            p.wait("vector", "DS", v2)
            p.instr("vector", lambda v: v.tensor_copy(wxp_nb2[:], wxp_nat[0:32, :]))
            p.wait("sync", "V", p.cur("V"))
            v3 = p.dma("sync", wdt_nat[:].rearrange("p (kt r) -> p kt r", kt=NKT), wdt_in.rearrange("(kt p) r -> p kt r", p=128))
            p.wait("vector", "DS", v3)
            p.instr("vector", lambda v: v.tensor_copy(wdt_nb[:], wdt_nat[:]))
            V_WCAST = p.cur("V")
            p.wait("tensor", "V", V_WCAST)
            for kt in range(KTF):
                p.raw("tensor", lambda t, kt=kt: t.transpose(
                    ps_trb[:, 0:128], wxp_nb[:, kt * 128:(kt + 1) * 128], id16[:]))
                te = p.instr("tensor", lambda t, kt=kt: t.transpose(
                    ps_trb[:, 128:160], wxp_nb2[:, kt * 128:(kt + 1) * 128], id16[0:32, 0:32]))
                p.wait("scalar", "TE", te)
                p.instr("scalar", lambda s, kt=kt: s.activation(
                    out=wxpT[:, kt * NMIX:kt * NMIX + 128], in_=ps_trb[:, 0:128], func=AF.Copy))
                se = p.instr("scalar", lambda s, kt=kt: s.activation(
                    out=wxpT[:, kt * NMIX + 128:(kt + 1) * NMIX], in_=ps_trb[:, 128:160], func=AF.Copy))
                p.wait("tensor", "S", se)
            for kt in range(NKT):
                te = p.instr("tensor", lambda t, kt=kt: t.transpose(
                    ps_trb[:, 0:128], wdt_nb[:, kt * RNK:(kt + 1) * RNK], id16[:]))
                p.wait("scalar", "TE", te)
                se = p.instr("scalar", lambda s, kt=kt: s.activation(
                    out=wdtT[:, kt * 128:(kt + 1) * 128], in_=ps_trb[:, 0:128], func=AF.Copy))
                p.wait("tensor", "S", se)

        if P0_CUT == 1:
            for _tt in range(NTT):
                p.dma("gpsimd", out_t[_tt * 128:(_tt + 1) * 128, :],
                      x_in[_tt * 128:(_tt + 1) * 128, :])
            p.wait("gpsimd", "DG", p.cur("DG"))
            p.wait("sync", "DG", p.cur("DG"))
            raise _PhaseDone()

        if DBG == "wdtT":
            p.wait("gpsimd", "S", p.cur("S"))
            p.dma("gpsimd", out_t.flatten()[0:128 * NKT * 128].rearrange("(p n) -> p n", p=128),
                  wdtT[:])
            p.dma("gpsimd", out_t.flatten()[128 * NKT * 128:2 * 128 * NKT * 128]
                  .rearrange("(p n) -> p n", p=128), wxpT[:, 0:NKT * 128])
            p.wait("gpsimd", "DG", p.cur("DG"))
            p.wait("sync", "DG", p.cur("DG"))
            raise _PhaseDone()

        inv_d = 1.0 / D
        ph0 = {"dg_xnT": 0, "te_xp": 0, "vv_xp": 0}
        for tt in range(NTT):
            vx = p.dma("sync", xt[:], x_in[tt * 128:(tt + 1) * 128, :])
            p.wait("vector", "DS", vx)
            p.instr("vector", lambda v: v.reduce_sum(st_a[:, 0:1], xt[:], axis=AX.X))
            p.wait("scalar", "DS", vx)
            p.instr("scalar", lambda s: s.activation(
                out=sq_scr[:], in_=xt[:], func=AF.Square, accum_out=st_a[:, 1:2]))
            p.wait("vector", "S", p.cur("S"))
            p.wait("vector", "V", p.cur("V"))
            p.instr("vector", lambda v: v.tensor_scalar(
                out=st_a[:, 2:3], in0=st_a[:, 0:1], scalar1=-inv_d, scalar2=None, op0=OP.mult))
            p.instr("vector", lambda v: v.tensor_scalar(
                out=st_a[:, 3:4], in0=st_a[:, 1:2], scalar1=inv_d, scalar2=None, op0=OP.mult))
            p.wait("vector", "V", p.cur("V"))
            p.instr("vector", lambda v: v.tensor_tensor(
                out=st_a[:, 4:5], in0=st_a[:, 2:3], in1=st_a[:, 2:3], op=OP.mult))
            p.wait("vector", "V", p.cur("V"))
            vv = p.instr("vector", lambda v: v.tensor_tensor(
                out=st_a[:, 5:6], in0=st_a[:, 3:4], in1=st_a[:, 4:5], op=OP.subtract))
            p.wait("scalar", "V", vv)
            p.instr("scalar", lambda s: s.activation(
                out=st_a[:, 6:7], in_=st_a[:, 5:6], func=AF.Ln, bias=eps_t[:, 0:1]))
            p.wait("scalar", "S", p.cur("S"))
            sv = p.instr("scalar", lambda s: s.activation(
                out=st_a[:, 7:8], in_=st_a[:, 6:7], func=AF.Exp, scale=-0.5))
            if DBG == "st0" and tt == 0:
                p.wait("gpsimd", "S", p.cur("S"))
                p.wait("gpsimd", "V", p.cur("V"))
                p.dma("gpsimd", out_t[0:128, 0:16], st_a[:])
                p.wait("gpsimd", "DG", p.cur("DG"))
                p.wait("sync", "DG", p.cur("DG"))
                raise _PhaseDone()
            p.wait("vector", "S", sv)
            p.wait("vector", "TE", ph0.get("te_xn", 0))
            vxn = p.instr("vector", lambda v: v.tensor_scalar(
                out=xnt[:], in0=xt[:], scalar1=st_a[:, 2:3], scalar2=st_a[:, 7:8],
                op0=OP.add, op1=OP.mult))
            if DBG == "xn":
                p.wait("gpsimd", "V", vxn)
                p.dma("gpsimd", out_t[tt * 128:(tt + 1) * 128, :], xnt[:])
                p.wait("vector", "DG", p.cur("DG"))
            p.wait("tensor", "V", vxn)
            p.wait("scalar", "DG", ph0["dg_xnT"])
            p.wait("scalar", "TE", ph0["te_xp"])
            for kb in range(KTF // 4):
                hb = (kb % 2) * 1024
                for j in range(4):
                    kt = kb * 4 + j
                    fn = (lambda t, kt=kt, j=j, hb=hb:
                          t.transpose(ps_trb[:, hb + j * 128:hb + (j + 1) * 128],
                                      xnt[:, kt * 128:(kt + 1) * 128], id16[:]))
                    if j == 3:
                        te = p.instr("tensor", fn)
                    else:
                        p.raw("tensor", fn)
                p.wait("scalar", "TE", te)
                se = p.instr("scalar", lambda s, kb=kb, hb=hb: s.activation(
                    out=xnT_tt[:, kb * 512:(kb + 1) * 512], in_=ps_trb[:, hb:hb + 512], func=AF.Copy))
                if kb >= 1:
                    p.wait("tensor", "S", p.cur("S") - 1)
            ph0["te_xn"] = te
            SE_XN = p.cur("S")
            p.wait("gpsimd", "S", SE_XN)
            for dsl in range(NCORE):
                p.dma("gpsimd", cin_xf[dsl, tt].rearrange("(ktl p) t -> p ktl t", p=128),
                      xnT_tt[:, dsl * 512:(dsl + 1) * 512].rearrange(
                          "p (ktl t) -> p ktl t", ktl=NKT))
            p.dma("gpsimd", cin_xg[tt], xnT_tt[:])
            ph0["dg_xnT"] = p.cur("DG")
            p.wait("tensor", "S", SE_XN)
            p.wait("tensor", "V", ph0["vv_xp"])
            for kt in range(KTF):
                fn = (lambda t, kt=kt: t.matmul(
                    ps_dt[:, 0:NMIX], xnT_tt[:, kt * 128:(kt + 1) * 128],
                    wxpT[:, kt * NMIX:(kt + 1) * NMIX],
                    start=(kt == 0), stop=(kt == KTF - 1)))
                if kt == KTF - 1:
                    te = p.instr("tensor", fn)
                else:
                    p.raw("tensor", fn)
            ph0["te_xp"] = te
            p.wait("vector", "TE", te)
            vv = p.instr("vector", lambda v: v.tensor_copy(xp_t[:], ps_dt[:, 0:NMIX]))
            ph0["vv_xp"] = vv
            p.wait("tensor", "V", vv)
            p.raw("tensor", lambda t: t.transpose(ps_trf[:, 0:128], xp_t[:, 0:RNK], id32[:]))
            p.raw("tensor", lambda t: t.transpose(ps_trf[0:NST, 128:256], xp_t[:, RNK:RNK + NST], id32[:]))
            te = p.instr("tensor", lambda t: t.transpose(ps_trf[0:NST, 256:384], xp_t[:, RNK + NST:NMIX], id32[:]))
            p.wait("scalar", "TE", te)
            p.instr("scalar", lambda s, tt=tt: s.activation(
                out=dtinT_loc[:, tt * 128:(tt + 1) * 128], in_=ps_trf[:, 0:128], func=AF.Copy))
            p.instr("scalar", lambda s, tt=tt: s.activation(
                out=bT_loc[:, tt * 128:(tt + 1) * 128], in_=ps_trf[0:NST, 128:256], func=AF.Copy))
            se = p.instr("scalar", lambda s, tt=tt: s.activation(
                out=cT_loc[:, tt * 128:(tt + 1) * 128], in_=ps_trf[0:NST, 256:384], func=AF.Copy))
            if DBG == "xp":
                p.wait("gpsimd", "V", p.cur("V"))
                p.dma("gpsimd", out_t[tt * 128:(tt + 1) * 128, 0:NMIX], xp_t[:])
                p.wait("vector", "DG", p.cur("DG"))
            p.wait("tensor", "S", se)
            p.wait("sync", "V", vxn)

        if P0_CUT == 2:
            for _tt in range(NTT):
                p.dma("gpsimd", out_t[_tt * 128:(_tt + 1) * 128, :],
                      x_in[_tt * 128:(_tt + 1) * 128, :])
            p.wait("gpsimd", "DG", p.cur("DG"))
            p.wait("sync", "DG", p.cur("DG"))
            raise _PhaseDone()

        p.wait("gpsimd", "S", p.cur("S"))
        p.dma("gpsimd", cin_ms[0:RNK, :], dtinT_loc[:])
        p.dma("gpsimd", cin_ms[RNK:RNK + NST, :], bT_loc[:])
        p.dma("gpsimd", cin_ms[RNK + NST:NMIX, :], cT_loc[:])
        p.wait("gpsimd", "DG", p.cur("DG"))
        CC_XN = p.cc(lambda g: g.collective_compute(
            "AllGather", OP.bypass, ins=[cin_xf], outs=[cout_xa], replica_groups=rg))
        CC_MS = p.cc(lambda g: g.collective_compute(
            "AllGather", OP.bypass, ins=[cin_ms], outs=[cout_ms], replica_groups=rg))

        if P0_CUT == 3:
            for _tt in range(NTT):
                p.dma("gpsimd", out_t[_tt * 128:(_tt + 1) * 128, :],
                      x_in[_tt * 128:(_tt + 1) * 128, :])
            p.wait("gpsimd", "DG", p.cur("DG"))
            p.wait("sync", "DG", p.cur("DG"))
            raise _PhaseDone()

        dq_prev_v = 0
        for i in range(NKT):
            if i > 0:
                p.wait("sync", "V", dq_prev_v)
                p.wait("gpsimd", "V", dq_prev_v)
            vpk = p.dma("sync", pk_t[:], wpk_in[i * 128:(i + 1) * 128, :])
            vsc = p.dma("gpsimd", sc_f[:], wsc_in[i * 128:(i + 1) * 128, :], sem="DB")
            p.wait("vector", "DS", vpk)
            p.wait("vector", "DB", vsc)
            p.instr("vector", lambda v: v.tensor_scalar(
                out=hi_i[:], in0=pk_t[:], scalar1=4, scalar2=15,
                op0=OP.logical_shift_right, op1=OP.bitwise_and))
            p.instr("vector", lambda v: v.tensor_scalar(
                out=pk_t[:], in0=pk_t[:], scalar1=15, scalar2=None, op0=OP.bitwise_and))
            p.wait("vector", "V", p.cur("V"))
            p.instr("vector", lambda v: v.tensor_scalar(
                out=lo_f[:], in0=pk_t[:], scalar1=-8.0, scalar2=None, op0=OP.add))
            p.instr("vector", lambda v: v.tensor_scalar(
                out=hi_f[:], in0=hi_i[:], scalar1=-8.0, scalar2=None, op0=OP.add))
            p.wait("vector", "V", p.cur("V"))
            for r in range(16):
                fnc = (lambda v, r=r: v.tensor_copy(
                    sc_exp[:].rearrange("p (b r) -> p b r", r=16)[:, :, r], sc_f[:]))
                if r == 15:
                    p.instr("vector", fnc)
                else:
                    p.raw("vector", fnc)
            p.wait("vector", "V", p.cur("V"))
            wv = w_bf[:].rearrange("p (j two) -> p j two", two=2)
            p.instr("vector", lambda v: v.tensor_tensor(
                out=wv[:, :, 0], in0=lo_f[:], in1=sc_exp[:], op=OP.mult))
            vdq = p.instr("vector", lambda v: v.tensor_tensor(
                out=wv[:, :, 1], in0=hi_f[:], in1=sc_exp[:], op=OP.mult))
            dq_prev_v = vdq
            p.wait("tensor", "V", vdq)
            for kb in range(KTF // 4):
                hb = (kb % 2) * 1024
                for j in range(4):
                    kt = kb * 4 + j
                    fn = (lambda t, kt=kt, j=j, hb=hb: t.transpose(
                        ps_trb[:, hb + j * 128:hb + (j + 1) * 128], w_bf[:, kt * 128:(kt + 1) * 128], id16[:]))
                    if j == 3:
                        te = p.instr("tensor", fn)
                    else:
                        p.raw("tensor", fn)
                p.wait("scalar", "TE", te)
                se = p.instr("scalar", lambda s, kb=kb, hb=hb: s.activation(
                    out=wT_pc[:, kb * 512:(kb + 1) * 512], in_=ps_trb[:, hb:hb + 512], func=AF.Copy))
                if kb >= 1:
                    p.wait("tensor", "S", p.cur("S") - 1)
            p.wait("gpsimd", "S", p.cur("S"))
            p.dma("gpsimd",
                  cin_w.rearrange("(ktg p) do -> p ktg do", p=128)[:, :, i * 128:(i + 1) * 128],
                  wT_pc[:].rearrange("p (ktg t) -> p ktg t", ktg=KTF))
            p.wait("vector", "DG", p.cur("DG"))
        p.wait("gpsimd", "DG", p.cur("DG"))
        CC_W = p.cc(lambda g: g.collective_compute(
            "AllGather", OP.bypass, ins=[cin_w], outs=[cout_w], replica_groups=rg))

        if P0_CUT == 4:
            for _tt in range(NTT):
                p.dma("gpsimd", out_t[_tt * 128:(_tt + 1) * 128, :],
                      x_in[_tt * 128:(_tt + 1) * 128, :])
            p.wait("gpsimd", "DG", p.cur("DG"))
            p.wait("sync", "DG", p.cur("DG"))
            raise _PhaseDone()

        es0.close()
        bb_1 = S(es1, "bb_all", [128, NST * LC], F16)
        cb_1 = S(es1, "cb_all", [128, NST * LC], F16)
        bb_d2 = [bb_1[:], bb_1[:]]
        cb_d2 = [cb_1[:], cb_1[:]]
        dt_c = S(es1, "dt_c", [128, LC], F32)
        z_c = S(es1, "z_c", [128, LC], F32)
        zc_c = S(es1, "zc_c", [128, LC], F32)
        u_c = S(es1, "u_c", [128, LC], F16)
        xn_2 = S(es1, "xn_all", [128, 2 * NKT * LC], F16)
        xn_d2 = [xn_2[:, 0:NKT * LC], xn_2[:, NKT * LC:2 * NKT * LC]]
        da_t = S(es1, "da_d", [128, 3 * LC], F32)
        da_d = [da_t[:, 0:LC], da_t[:, LC:2 * LC], da_t[:, 2 * LC:3 * LC]]
        dbx_t = S(es1, "dbx_d", [128, 2 * LC], F16)
        dbx_d = [dbx_t[:, 0:LC], dbx_t[:, LC:2 * LC]]
        h_t = S(es1, "h_d", [128, 2 * LC], F16)
        h_d = [h_t[:, 0:LC], h_t[:, LC:2 * LC]]
        yv_t = S(es1, "yv_d", [128, 2 * LC], F16)
        yv_d = [yv_t[:, 0:LC], yv_t[:, LC:2 * LC]]
        y_sb = S(es1, "y_sb", [128, LC], F32)
        yT_sb = S(es1, "yT_sb", [128, LC], F16)
        w_r = S(es1, "w_r", [128, KTF * 512], F16)
        xnT_ft = S(es1, "xnT_f", [128, 2 * KTF * 128], F16)
        xnT_f = [xnT_ft[:, 0:KTF * 128], xnT_ft[:, KTF * 128:2 * KTF * 128]]
        g_sbt = S(es1, "g_sb", [128, 1024], F16)
        g_sb = [g_sbt[:, 0:512], g_sbt[:, 512:1024]]

        if P0_CUT == 5:
            for _tt in range(NTT):
                p.dma("gpsimd", out_t[_tt * 128:(_tt + 1) * 128, :],
                      x_in[_tt * 128:(_tt + 1) * 128, :])
            p.wait("gpsimd", "DG", p.cur("DG"))
            p.wait("sync", "DG", p.cur("DG"))
            raise _PhaseDone()

        p.wait("sync", "CC", CC_MS)
        if P0_CUT != 7:
            for r in range(NCORE):
                p.dma("sync", dtinT[:, r * TL:(r + 1) * TL], cout_ms[r * NMIX:r * NMIX + RNK, :],
                      sem="DD")
        V_DTIN = p.cur("DD")
        if P0_CUT in (6, 7):
            for _tt in range(NTT):
                p.dma("gpsimd", out_t[_tt * 128:(_tt + 1) * 128, :],
                      x_in[_tt * 128:(_tt + 1) * 128, :])
            p.wait("gpsimd", "DG", p.cur("DG"))
            p.wait("sync", "DG", p.cur("DG"))
            raise _PhaseDone()

        p.instr("vector", lambda v: v.memset(carry[:], 0.0))
        V_CARRY = p.cur("V")

        if DBG in ("xn", "xp"):
            p.wait("gpsimd", "DG", p.cur("DG"))
            p.wait("sync", "DG", p.cur("DG"))
            raise _PhaseDone()
        if DBG == "cinxf":
            p.wait("gpsimd", "CC", CC_W)
            p.dma("gpsimd",
                  out_t.flatten().rearrange("(p n) -> p n", p=128),
                  cin_xf.flatten().rearrange("(p n) -> p n", p=128))
            p.wait("gpsimd", "DG", p.cur("DG"))
            p.wait("sync", "DG", p.cur("DG"))
            raise _PhaseDone()
        if DBG == "dtins":
            p.wait("gpsimd", "DD", V_DTIN)
            p.dma("gpsimd", out_t.flatten()[0:128 * T].rearrange("(p t) -> p t", p=128),
                  dtinT[:])
            p.wait("gpsimd", "DG", p.cur("DG"))
            p.wait("sync", "DG", p.cur("DG"))
            raise _PhaseDone()
        if DBG == "dtin":
            p.wait("gpsimd", "DS", V_DTIN)
            p.dma("gpsimd",
                  out_t.flatten()[0:128 * T].rearrange("(p t) -> p t", p=128),
                  dtinT[:])
            p.wait("gpsimd", "DG", p.cur("DG"))
            p.wait("sync", "DG", p.cur("DG"))
            raise _PhaseDone()
        if PHASE_LIMIT == 0:
            for tt in range(NTT):
                p.dma("gpsimd", out_t[tt * 128:(tt + 1) * 128, :],
                      x_in[tt * 128:(tt + 1) * 128, :])
            p.wait("gpsimd", "DG", p.cur("DG"))
            p.wait("sync", "DG", p.cur("DG"))
            raise _PhaseDone()

        p.wait("sync", "CC", CC_XN)
        s_ds0 = p.sem("DX")
        if STAGE_ON:
            p.counts["DX"] += 16 * NCORE
            def stage_xn(e, _s=s_ds0):
                pid = pid_cache["sy"]
                for ch in range(NCORE):
                    if ch > 0:
                        e.wait_ge(_s, 16 * ch)
                    e.dma_start(out=xn_loc[ch], in_=cout_xa[ch, bass.ds(pid, 1)]).then_inc(_s, 16)
            p.ops["sync"].append(stage_xn)
        V_XNSTAGE = p.cur("DX")

        ffn = {"r": 0, "tt": 0, "w_loaded": -1, "done": False,
               "bank_evac": [0, 0],
               "gsb_store": [0, 0],
               "xnf_mm": [0, 0],
               "w_v": 0}

        def emit_ffn_step():
            if ffn["done"] or not FFN_ON:
                return
            r, tt = ffn["r"], ffn["tt"]
            if tt == 0 and ffn["w_loaded"] != r:
                p.wait("sync", "CC", CC_W)
                if r > 0:
                    p.wait("sync", "TE", p.cur("TE"))
                p.dma("sync", w_r[:].rearrange("p (ktg do) -> p ktg do", ktg=KTF),
                      cout_w[r * D:(r + 1) * D, :].rearrange("(ktg p) do -> p ktg do", p=128),
                      sem="DWR")
                ffn["w_v"] = p.cur("DWR")
                ffn["w_loaded"] = r
            buf = tt % 2
            p.wait("sync", "TE", ffn["xnf_mm"][buf])
            if DBG == "xnf" and r == 0 and tt == 0:
                sv_ = p.instr("vector", lambda v, buf=buf: v.memset(xnT_f[buf], 7.0))
                p.wait("sync", "V", sv_)
            p.dma("sync", xnT_f[buf], cin_xg[tt], sem="DW0" if buf == 0 else "DW1")
            v_xn = p.cur("DW0" if buf == 0 else "DW1")
            if DBG == "wdq" and r == 0 and tt == 0:
                p.wait("gpsimd", "DWR", ffn["w_v"])
                p.dma("gpsimd",
                      out_t.flatten()[0:128 * KTF * 512].rearrange("(p n) -> p n", p=128),
                      w_r[:])
            p.wait("tensor", "DWR", ffn["w_v"])
            p.wait("tensor", "DW0" if buf == 0 else "DW1", v_xn)
            if DBG == "xnf" and r == 0 and tt == 0:
                p.wait("gpsimd", "DS", v_xn)
                p.dma("gpsimd",
                      out_t.flatten()[0:128 * KTF * 128].rearrange("(p n) -> p n", p=128),
                      xnT_f[buf])
            bank = ps_ffn[:, buf * 512:(buf + 1) * 512]
            p.wait("tensor", "S", ffn["bank_evac"][buf])
            for kt in range(KTF):
                fn = (lambda t, kt=kt, buf=buf, bank=bank: t.matmul(
                    bank, xnT_f[buf][:, kt * 128:(kt + 1) * 128],
                    w_r[:, kt * 512:(kt + 1) * 512],
                    start=(kt == 0), stop=(kt == KTF - 1)))
                if kt == KTF - 1:
                    te = p.instr("tensor", fn)
                else:
                    p.raw("tensor", fn)
            ffn["xnf_mm"][buf] = te
            p.wait("scalar", "TE", te)
            p.wait("scalar", "DG0" if buf == 0 else "DG1", ffn["gsb_store"][buf])
            se = p.instr("scalar", lambda s, buf=buf, bank=bank: s.activation(
                out=g_sb[buf], in_=bank, func=AF.Silu))
            ffn["bank_evac"][buf] = se
            p.wait("gpsimd", "S", se)
            p.dma("gpsimd", g_buf[tt * 128:(tt + 1) * 128, r * 512:(r + 1) * 512], g_sb[buf],
                  sem="DG0" if buf == 0 else "DG1")
            ffn["gsb_store"][buf] = p.cur("DG0" if buf == 0 else "DG1")
            ffn["tt"] += 1
            if ffn["tt"] == NTT:
                ffn["tt"] = 0
                ffn["r"] += 1
                if ffn["r"] == NCORE:
                    ffn["done"] = True

        p.wait("vector", "V", V_CARRY)
        scan_sem = {}
        nctr = [0]
        ymm_sem = [0, 0]
        yfin_state = {"vv_y": 0, "te_tr": 0, "dg_y": 0, "da_last": 0}
        def load_bc(chl):
            p.dma("gpsimd", bb_d2[chl % 2][:].rearrange("p (n t) -> p n t", n=NST),
                  cout_ms[chl * NMIX + RNK:chl * NMIX + RNK + NST, :].partition_broadcast(128),
                  sem="DB")
            p.dma("sync", cb_d2[chl % 2][:].rearrange("p (n t) -> p n t", n=NST),
                  cout_ms[chl * NMIX + RNK + NST:chl * NMIX + NMIX, :].partition_broadcast(128),
                  sem="DB2")
            return (p.cur("DB"), p.cur("DB2"))
        def load_xn(chl):
            for i in range(NTT):
                p.dma("scalar",
                      xn_d2[chl % 2][:].rearrange("p (ktl i t) -> p ktl i t",
                                                  ktl=NKT, i=NTT)[:, :, i, :],
                      xn_loc[chl, i].rearrange("(ktl p) t -> p ktl t", p=128))
            return p.cur("DA")
        v_bc_of = {}
        v_xn_of = {}
        v_chend = [0, 0]
        for ch in range(NCH_LIM):
            if ch == 0:
                p.wait("gpsimd", "CC", CC_MS)
                p.wait("gpsimd", "CC", CC_W)
                p.wait("sync", "CC", CC_MS)
                p.wait("sync", "CC", CC_W)
                p.wait("scalar", "CC", CC_XN)
                p.wait("scalar", "CC", CC_W)
                p.wait("scalar", "DX", V_XNSTAGE)
                v_xn_of[0] = load_xn(0)
            else:
                p.wait("gpsimd", "V", p.cur("V"))
                p.wait("sync", "V", p.cur("V"))
            v_bc_of[ch] = load_bc(ch)
            if ch + 1 < NCH_LIM:
                p.wait("scalar", "V", v_chend[(ch + 1) % 2])
                v_xn_of[ch + 1] = load_xn(ch + 1)
            bb_all = bb_d2[ch % 2]
            cb_all = cb_d2[ch % 2]
            xn_all = xn_d2[ch % 2]
            V_BC = v_bc_of[ch]
            V_XNC = v_xn_of[ch]
            for kt in range(NKT):
                p.wait("tensor", "DD", V_DTIN)
                for q in range(2):
                    te = p.instr("tensor", lambda t, kt=kt, ch=ch, q=q: t.matmul(
                        ps_dt[:, 0:512], wdtT[:, kt * 128:(kt + 1) * 128],
                        dtinT[:, ch * LC + q * 512:ch * LC + (q + 1) * 512],
                        start=True, stop=True))
                    p.wait("vector", "TE", te)
                    if q == 0:
                        p.wait("vector", "S", yfin_state["da_last"])
                    vv = p.instr("vector", lambda v, q=q, kt=kt: v.tensor_scalar(
                        out=z_c[:, q * 512:(q + 1) * 512], in0=ps_dt[:, 0:512],
                        scalar1=dtb_par[:, kt:kt + 1], scalar2=None, op0=OP.add))
                    if (DBG or "").startswith("zone") and q == 0 and                             (ch, kt) == tuple(int(v) for v in (DBG.split(":")[1:3] or [0, 0])):
                        p.wait("gpsimd", "V", p.cur("V"))
                        p.dma("gpsimd", out_t.flatten()[0:65536].rearrange("(p t) -> p t", p=128), z_c[:, 0:512])
                        p.dma("gpsimd", out_t.flatten()[65536:131072].rearrange("(p t) -> p t", p=128), dtinT[:, ch * LC:ch * LC + 512])
                        p.dma("gpsimd", out_t.flatten()[131072:147456].rearrange("(p t) -> p t", p=128), wdtT[:, kt * 128:(kt + 1) * 128])
                        p.dma("gpsimd", out_t.flatten()[147456:147584].rearrange("(p t) -> p t", p=128), dtb_par[:, kt:kt + 1])
                        p.wait("gpsimd", "DG", p.cur("DG"))
                        p.wait("sync", "DG", p.cur("DG"))
                        raise _PhaseDone()
                    p.wait("tensor", "V", vv)
                p.instr("vector", lambda v: v.tensor_scalar(
                    out=zc_c[:], in0=z_c[:], scalar1=20.0, scalar2=None, op0=OP.min))
                p.wait("scalar", "V", p.cur("V"))
                p.instr("scalar", lambda s: s.activation(out=zc_c[:], in_=zc_c[:], func=AF.Exp))
                se = p.instr("scalar", lambda s: s.activation(
                    out=zc_c[:], in_=zc_c[:], func=AF.Ln, bias=1.0))
                p.wait("vector", "S", se)
                p.instr("vector", lambda v: v.tensor_tensor(
                    out=dt_c[:], in0=zc_c[:], in1=z_c[:], op=OP.max))
                if (DBG or "").startswith("dtone") and                         (ch, kt) == tuple(int(v) for v in (DBG.split(":")[1:3] or [0, 0])):
                    p.wait("gpsimd", "V", p.cur("V"))
                    p.dma("gpsimd", out_t.flatten()[0:131072].rearrange("(p t) -> p t", p=128), dt_c[:])
                    p.dma("gpsimd", out_t.flatten()[131072:262144].rearrange("(p t) -> p t", p=128), z_c[:])
                    p.wait("gpsimd", "DG", p.cur("DG"))
                    p.wait("sync", "DG", p.cur("DG"))
                    raise _PhaseDone()
                if DBG == "zraw":
                    p.wait("gpsimd", "V", p.cur("V"))
                    offz = (ch * NKT + kt) * 128 * 1024
                    p.dma("gpsimd",
                          out_t.flatten()[offz:offz + 131072].rearrange("(p t) -> p t", p=128),
                          z_c[:])
                    p.wait("vector", "DG", p.cur("DG"))
                if DBG == "dt":
                    p.wait("gpsimd", "V", p.cur("V"))
                    off = (ch * NKT + kt) * 128 * 1024
                    p.dma("gpsimd",
                          out_t.flatten()[off:off + 131072].rearrange("(p t) -> p t", p=128),
                          dt_c[:])
                    p.wait("vector", "DG", p.cur("DG"))
                p.wait("vector", "DA", V_XNC)
                p.instr("vector", lambda v, kt=kt, xn_all=xn_all: v.tensor_tensor(
                    out=u_c[:], in0=dt_c[:], in1=xn_all[:, kt * LC:(kt + 1) * LC],
                    op=OP.mult))
                V_U = p.cur("V")
                p.wait("scalar", "V", V_U)
                p.wait("vector", "DB", V_BC[0])
                p.wait("vector", "DB2", V_BC[1])
                for n in range(NST):
                    par = n % 2
                    apar = nctr[0] % 3
                    nctr[0] += 1
                    if apar in scan_sem:
                        p.wait("scalar", "V", scan_sem[apar])
                    se_da = p.instr("scalar", lambda s, kt=kt, n=n, apar=apar: s.activation(
                        out=da_d[apar], in_=dt_c[:], func=AF.Exp,
                        scale=a_par[:, kt * NST + n:kt * NST + n + 1]))
                    vv_dbx = p.instr("vector", lambda v, n=n, par=par, bb_all=bb_all: v.tensor_tensor(
                        out=dbx_d[par], in0=u_c[:], in1=bb_all[:, n * LC:(n + 1) * LC],
                        op=OP.mult))
                    p.wait("vector", "S", se_da)
                    vv_scan = p.instr("vector", lambda v, kt=kt, n=n, par=par, apar=apar: v.tensor_tensor_scan(
                        out=h_d[par], data0=da_d[apar], data1=dbx_d[par],
                        initial=carry[:, kt * NST + n:kt * NST + n + 1],
                        op0=OP.mult, op1=OP.add))
                    scan_sem[apar] = vv_scan
                    p.wait("vector", "V", vv_scan)
                    p.instr("vector", lambda v, kt=kt, n=n, par=par: v.tensor_copy(
                        carry[:, kt * NST + n:kt * NST + n + 1], h_d[par][:, LC - 1:LC]))
                    p.wait("vector", "TE", ymm_sem[par])
                    vv_yv = p.instr("vector", lambda v, n=n, par=par, cb_all=cb_all: v.tensor_tensor(
                        out=yv_d[par], in0=h_d[par], in1=cb_all[:, n * LC:(n + 1) * LC],
                        op=OP.mult))
                    p.wait("tensor", "V", vv_yv)
                    if n == 0:
                        p.wait("tensor", "V", yfin_state["vv_y"])
                    for q in range(2):
                        fn = (lambda t, q=q, n=n, par=par: t.matmul(
                            ps_y[:, q * 512:(q + 1) * 512], id16[:],
                            yv_d[par][:, q * 512:(q + 1) * 512],
                            start=(n == 0), stop=(n == NST - 1)))
                        if q == 1:
                            te_y = p.instr("tensor", fn)
                        else:
                            p.raw("tensor", fn)
                    ymm_sem[par] = te_y
                yfin_state["da_last"] = p.cur("S")
                p.wait("vector", "TE", te_y)
                p.wait("vector", "TE", yfin_state["te_tr"])
                p.instr("vector", lambda v, kt=kt, xn_all=xn_all: v.tensor_scalar(
                    out=y_sb[:], in0=xn_all[:, kt * LC:(kt + 1) * LC],
                    scalar1=d_par[:, kt:kt + 1], scalar2=None, op0=OP.mult))
                vv_y = p.instr("vector", lambda v: v.tensor_tensor(
                    out=y_sb[:], in0=y_sb[:], in1=ps_y[:], op=OP.add))
                yfin_state["vv_y"] = vv_y
                if (DBG or "").startswith("yone") and                         (ch, kt) == tuple(int(v) for v in (DBG.split(":")[1:3] or [0, 0])):
                    p.wait("gpsimd", "V", p.cur("V"))
                    p.dma("gpsimd", out_t.flatten()[0:131072].rearrange("(p t) -> p t", p=128), y_sb[:])
                    p.dma("gpsimd", out_t.flatten()[131072:262144].rearrange("(p t) -> p t", p=128), dt_c[:])
                    p.dma("gpsimd", out_t.flatten()[262144:262144 + 16384].rearrange("(p t) -> p t", p=128),
                          bb_all[:, 0:128])
                    p.wait("gpsimd", "DG", p.cur("DG"))
                    p.wait("sync", "DG", p.cur("DG"))
                    raise _PhaseDone()
                p.wait("tensor", "V", vv_y)
                for half in range(2):
                    if half == 1:
                        p.wait("tensor", "S", p.cur("S"))
                    for j in range(4):
                        i = half * 4 + j
                        fn = (lambda t, i=i, j=j: t.transpose(
                            ps_trf[:, j * 128:(j + 1) * 128],
                            y_sb[:, i * 128:(i + 1) * 128], id32[:]))
                        if j == 3:
                            te = p.instr("tensor", fn)
                        else:
                            p.raw("tensor", fn)
                    p.wait("scalar", "TE", te)
                    if half == 0:
                        p.wait("scalar", "DY", yfin_state["dg_y"])
                    se = p.instr("scalar", lambda s, half=half: s.activation(
                        out=yT_sb[:, half * 512:(half + 1) * 512], in_=ps_trf[:, 0:512],
                        func=AF.Copy))
                yfin_state["te_tr"] = te
                p.wait("gpsimd", "S", se)
                p.dma("gpsimd", cin_y[ch, :, kt].rearrange("i p q -> p i q"),
                      yT_sb[:].rearrange("p (i q) -> p i q", i=NTT), sem="DY")
                yfin_state["dg_y"] = p.cur("DY")
                if (DBG or "").startswith("ycin") and                         (ch, kt) == tuple(int(v) for v in (DBG.split(":")[1:3] or [0, 0])):
                    p.wait("gpsimd", "DG", p.cur("DG"))
                    for i_ in range(NTT):
                        p.dma("gpsimd",
                              out_t.flatten()[i_ * 16384:(i_ + 1) * 16384]
                              .rearrange("(p d) -> p d", p=128),
                              cin_y[ch, i_, kt])
                    p.dma("gpsimd", out_t.flatten()[131072:262144].rearrange("(p t) -> p t", p=128),
                          yT_sb[:])
                    p.wait("gpsimd", "DG", p.cur("DG"))
                    p.wait("sync", "DG", p.cur("DG"))
                    raise _PhaseDone()
                if (DBG or "").startswith("yat") and                         (ch, kt) == tuple(int(v) for v in DBG.split(":")[1:3]):
                    p.wait("gpsimd", "DY", p.cur("DY"))
                    for i_ in range(NTT):
                        for k_ in range(NKT):
                            off_ = (i_ * NKT + k_) * 16384
                            p.dma("gpsimd", out_t.flatten()[off_:off_ + 16384]
                                  .rearrange("(p d) -> p d", p=128),
                                  cin_y[0, i_, k_])
                    p.wait("gpsimd", "DG", p.cur("DG"))
                    p.wait("sync", "DG", p.cur("DG"))
                    raise _PhaseDone()
                p.wait("tensor", "S", p.cur("S"))
                emit_ffn_step()
                emit_ffn_step()
                if kt == NKT - 1:
                    v_chend[ch % 2] = p.cur("V")
                    p.wait("gpsimd", "DY", p.cur("DY"))
                    cc_y_last = p.cc(lambda g, ch=ch: g.collective_compute(
                        "AllGather", OP.bypass, ins=[cin_y[ch]], outs=[cout_ya[ch]],
                        replica_groups=rg))
        while FFN_ON and not ffn["done"]:
            emit_ffn_step()
        p.wait("gpsimd", "DG0", p.cur("DG0"))
        p.wait("gpsimd", "DG1", p.cur("DG1"))
        p.wait("gpsimd", "DG", p.cur("DG"))
        CC_Y = cc_y_last

        es1.close()
        xt_c2 = S(es2, "xt2", [128, 2 * D], F32)
        g_t2 = S(es2, "g_t", [128, 2 * D], F16)
        sT_in2 = S(es2, "sT_in", [128, 2 * D], F16)
        o_t2 = S(es2, "o_t", [128, 2 * D], F32)
        xt_c = xt_c2[:, 0:D]
        

        if DBG in ("wdq", "xnf"):
            p.wait("gpsimd", "DG", p.cur("DG"))
            p.wait("sync", "DG", p.cur("DG"))
            raise _PhaseDone()
        if DBG in ("dt", "zraw"):
            p.wait("gpsimd", "DG", p.cur("DG"))
            p.wait("sync", "DG", p.cur("DG"))
            raise _PhaseDone()
        if DBG == "g":
            for _tt in range(NTT):
                p.dma("gpsimd", out_t[_tt * 128:(_tt + 1) * 128, :],
                      g_buf[_tt * 128:(_tt + 1) * 128, :])
            p.wait("gpsimd", "DG", p.cur("DG"))
            p.wait("sync", "DG", p.cur("DG"))
            raise _PhaseDone()
        if PHASE_LIMIT == 1:
            for tt in range(NTT):
                p.dma("gpsimd", out_t[tt * 128:(tt + 1) * 128, :],
                      x_in[tt * 128:(tt + 1) * 128, :])
            p.wait("gpsimd", "CC", CC_Y)
            p.wait("gpsimd", "DG", p.cur("DG"))
            p.wait("sync", "DG", p.cur("DG"))
            raise _PhaseDone()

        if DBG == "ycfin":
            p.wait("gpsimd", "CC", CC_Y)
            p.dma("gpsimd", out_t.flatten()[0:NTT * NKT * 128 * 128]
                  .rearrange("(a b) -> a b", b=128),
                  cin_y[0].rearrange("i k p d -> (i k p) d"))
            p.wait("gpsimd", "DG", p.cur("DG"))
            p.wait("sync", "DG", p.cur("DG"))
            raise _PhaseDone()

        p.wait("sync", "CC", CC_Y)
        p.wait("sync", "G", p.cur("G"))
        p.wait("scalar", "CC", CC_Y)
        s_ds1 = p.sem("DS")
        ds_base_y = p.counts["DS"]
        p.counts["DS"] += 16
        def stage_y_lo(e, _s=s_ds1, _b=ds_base_y):
            pid = pid_cache["sy"]
            if _b > 0:
                e.wait_ge(_s, _b)
            e.dma_start(out=s_loc[0:4], in_=cout_ya[bass.ds(pid, 1), 0:4]).then_inc(_s, 16)
        p.ops["sync"].append(stage_y_lo)
        s_da1 = p.sem("DA")
        da_base_y = p.counts["DA"]
        p.counts["DA"] += 16
        def stage_y_hi(e, _s=s_da1, _b=da_base_y):
            pid = pid_cache["sc"]
            if _b > 0:
                e.wait_ge(_s, _b)
            e.dma_start(out=s_loc[4:8], in_=cout_ya[bass.ds(pid, 1), 4:8]).then_inc(_s, 16)
        p.ops["scalar"].append(stage_y_hi)
        vv_hist = [0, 0]
        dg_hist = [0, 0]
        ld_hist = []
        for tt in range(NTT):
            buf = tt % 2
            xt_b = xt_c2[:, buf * D:(buf + 1) * D]
            g_b = g_t2[:, buf * D:(buf + 1) * D]
            sT_b = sT_in2[:, buf * D:(buf + 1) * D]
            o_b = o_t2[:, buf * D:(buf + 1) * D]
            p.wait("sync", "V", vv_hist[buf])
            p.wait("scalar", "V", vv_hist[buf])
            p.wait("gpsimd", "V", vv_hist[buf])
            for r in range(4):
                p.dma("sync",
                      sT_b[:, r * 512:(r + 1) * 512].rearrange("p (ktl t) -> p ktl t", ktl=NKT),
                      s_loc[r, tt].rearrange("ktl p t -> p ktl t"))
            vs_ds = p.cur("DS")
            for r in range(4, NCORE):
                p.dma("scalar",
                      sT_b[:, r * 512:(r + 1) * 512].rearrange("p (ktl t) -> p ktl t", ktl=NKT),
                      s_loc[r, tt].rearrange("ktl p t -> p ktl t"))
            vx_da = p.dma("scalar", xt_b, x_in[tt * 128:(tt + 1) * 128, :])
            vg_dg = p.dma("gpsimd", g_b, g_buf[tt * 128:(tt + 1) * 128, :], sem="DG2")
            p.wait("vector", "DS", vs_ds)
            p.wait("vector", "DA", vx_da)
            p.wait("vector", "DG2", vg_dg)
            p.wait("vector", "DG", dg_hist[buf])
            if DBG == "s":
                vv = p.instr("vector", lambda v, sT_b=sT_b, o_b=o_b: v.tensor_copy(o_b, sT_b))
            else:
                p.instr("vector", lambda v, sT_b=sT_b, g_b=g_b, o_b=o_b: v.tensor_tensor(
                    out=o_b, in0=sT_b, in1=g_b, op=OP.mult))
                vv = p.instr("vector", lambda v, o_b=o_b, xt_b=xt_b: v.tensor_tensor(
                    out=o_b, in0=o_b, in1=xt_b, op=OP.add))
            vv_hist[buf] = vv
            p.wait("gpsimd", "V", vv)
            p.dma("gpsimd", out_t[tt * 128:(tt + 1) * 128, :], o_b)
            dg_hist[buf] = p.cur("DG")
        p.wait("gpsimd", "DG", p.cur("DG"))
        p.wait("sync", "DG", p.cur("DG"))

        es2.close()
     except _PhaseDone:
        es2.close()
        es1.close()
        es0.close()

     if True:
        @block.sync
        def _(sync):
            for fn in p.ops["sync"]:
                fn(nc.sync)

        @block.vector
        def _(vector):
            for fn in p.ops["vector"]:
                fn(nc.vector)

        @block.scalar
        def _(scalar):
            for fn in p.ops["scalar"]:
                fn(nc.scalar)

        @block.tensor
        def _(tensor):
            for fn in p.ops["tensor"]:
                fn(nc.tensor)

        @block.gpsimd
        def _(gpsimd):
            for fn in p.ops["gpsimd"]:
                fn(nc.gpsimd)

    return nc


_NC_CACHE = None


def kernel(L0_attn_x, L0_attn_A_log, L0_attn_D, L0_attn_x_proj_w,
           L0_attn_dt_proj_w, L0_attn_dt_proj_b, L0_ffn_weight_packed,
           L0_ffn_scales):
    global _NC_CACHE
    x = np.ascontiguousarray(np.asarray(L0_attn_x, dtype=np.float32).reshape(T, D))
    alog = np.ascontiguousarray(np.asarray(L0_attn_A_log, dtype=np.float32))
    dvec = np.ascontiguousarray(np.asarray(L0_attn_D, dtype=np.float32))
    wxp = np.ascontiguousarray(np.asarray(L0_attn_x_proj_w, dtype=np.float32))
    wdt = np.ascontiguousarray(np.asarray(L0_attn_dt_proj_w, dtype=np.float32))
    dtb = np.ascontiguousarray(np.asarray(L0_attn_dt_proj_b, dtype=np.float32))
    wpk = np.ascontiguousarray(np.asarray(L0_ffn_weight_packed, dtype=np.int32))
    wsc = np.ascontiguousarray(np.asarray(L0_ffn_scales, dtype=np.float16))

    if _NC_CACHE is None:
        _NC_CACHE = build_program()
    nc = _NC_CACHE

    in_maps = []
    for c in range(NCORE):
        in_maps.append({
            "x_in": x[c * TL:(c + 1) * TL],
            "alog_in": alog[c * DS:(c + 1) * DS],
            "dvec_in": dvec[c * DS:(c + 1) * DS],
            "wxp_in": wxp,
            "wdt_in": wdt[c * DS:(c + 1) * DS],
            "dtb_in": dtb[c * DS:(c + 1) * DS],
            "wpk_in": wpk[c * DS:(c + 1) * DS],
            "wsc_in": wsc[c * DS:(c + 1) * DS],
        })
    res = run_bass_kernel_spmd(nc, in_maps, core_ids=list(range(NCORE)))
    out = np.concatenate([res.results[c]["out_t"] for c in range(NCORE)], axis=0)
    return out.reshape(1, T, D)
